# version 09
# baseline (speedup 1.0000x reference)
import numpy as np
from contextlib import ExitStack

import concourse.bass as bass
import concourse.mybir as mybir
from concourse.bass_utils import run_bass_kernel_spmd

F32 = mybir.dt.float32
BF16 = mybir.dt.bfloat16
AF = mybir.ActivationFunctionType
ALU = mybir.AluOpType
AX = mybir.AxisListType

FULL = dict(N=4096, D=2048, H=16, NE=7, TOPK=3, DFF=2048)
SMALL = dict(N=2048, D=512, H=4, NE=3, TOPK=2, DFF=512)

HD = 128
EPS_LN = 1e-5
EPS_L2 = 1e-6
ROPE_BASE = 10000.0


class Sem:
    def __init__(self, nc, name):
        self.h = nc.semaphore(name).__enter__()
        self.n = 0

    def inc(self, instr, k=1):
        if isinstance(instr, int):
            return instr
        instr.then_inc(self.h, k)
        self.n += k
        return self.n


class SyncEng:

    def __init__(self, eng, sem):
        object.__setattr__(self, "_e", eng)
        object.__setattr__(self, "_s", sem)

    def wait_ge(self, h, v):
        return self._e.wait_ge(h, v)

    def __getattr__(self, name):
        f = getattr(self._e, name)

        def wrap(*a, **kw):
            self._e.wait_ge(self._s.h, self._s.n)
            instr = f(*a, **kw)
            return self._s.inc(instr)

        return wrap


class _StopBuild(Exception):
    pass


def build_program(cfg, stop_phase=None, dbg=False):
    N, D, H, NE, TOPK, DFF = (cfg["N"], cfg["D"], cfg["H"], cfg["NE"],
                              cfg["TOPK"], cfg["DFF"])
    NQ = N // 2
    NT = N // 128
    NQT = NQ // 128
    DT = D // 128
    FT = DFF // 128
    QC = NQ // 256
    HB = H * 128
    SCALE = HD ** -0.5
    assert H * HD == D and DT == H

    nc = bass.Bass("TRN2", target_bir_lowering=False)

    dI = lambda nm, sh: nc.dram_tensor(nm, sh, F32, kind="ExternalInput").ap()
    dO = lambda nm, sh: nc.dram_tensor(nm, sh, F32, kind="ExternalOutput").ap()
    if dbg:
        dS = lambda nm, sh, dt=BF16: nc.dram_tensor(nm, sh, dt, kind="ExternalOutput").ap()
    else:
        dS = lambda nm, sh, dt=BF16: nc.dram_tensor(nm, sh, dt).ap()

    x_full = dI("x_full", [N, D])
    x_q = dI("x_q", [NQ, D])
    wq_in = dI("wq_in", [D, D])
    wk_in = dI("wk_in", [D, D])
    wv_in = dI("wv_in", [D, D])
    wo_in = dI("wo_in", [D, D])
    wb_in = dI("wb_in", [D, H])
    w1_in = dI("w1_in", [NE, D, DFF])
    wg_in = dI("wg_in", [NE, D, DFF])
    w2_in = dI("w2_in", [NE, DFF, D])
    ws1_in = dI("ws1_in", [D, DFF])
    wsg_in = dI("wsg_in", [D, DFF])
    ws2_in = dI("ws2_in", [DFF, D])
    gatew_in = dI("gatew_in", [D, NE])
    gateb_bc = dI("gateb_bc", [128, NE])
    biases_bc = dI("biases_bc", [128, NE])
    cos_kv = dI("cos_kv", [N, H * 64])
    sin_kv = dI("sin_kv", [N, H * 64])
    cos_q = dI("cos_q", [NQ, H * 64])
    sin_q = dI("sin_q", [NQ, H * 64])
    ident_in = dI("ident_in", [128, 128])
    tmasks_in = dI("tmasks_in", [7, 128, HB])
    irep_in = dI("irep_in", [128, HB])
    tworep_in = dI("tworep_in", [128, HB])
    pal_in = dI("pal_in", [4, 128, 256])

    out_sh = dO("out_sh", [NQ, D])

    q_sp = dS("q_sp", [NQT, DT, 128, 128])
    ydm_sp = dS("ydm_sp", [NT, DT, 128, 128])
    kdm_sp = dS("kdm_sp", [NT, H, 128, 128])
    ktdm_sp = dS("ktdm_sp", [NT, H, 128, 128])
    kttm_sp = dS("kttm_sp", [NT, 128, D])
    vtm_sp = dS("vtm_sp", [NT, 128, D])
    vn_sp = dS("vn_sp", [NT, H, 128, 128])
    x1_sp = dS("x1_sp", [NQ, D], F32)
    if dbg:
        dbg_hbuf = dS("dbg_hbuf", [128, (DFF // 128) * (NQ // 2)], BF16)
        dbg_outacc = dS("dbg_outacc", [128, DT * (NQ // 2)], F32)
        dbg_wrow = dS("dbg_wrow", [1, NE * (NQ // 2)], BF16)
        dbg_wbc = dS("dbg_wbc", [128, NQ // 2], BF16)
        dbg_gw = dS("dbg_gw", [128, DT * NE], BF16)
        dbg_gw0 = dS("dbg_gw0", [128, DT * NE], BF16)
        dbg_wtm = dS("dbg_wtm", [(NQ // 2) // 128, 128, NE], F32)
        dbg_gsc = dS("dbg_gsc", [(NQ // 2) // 128, 128, 4 * NE + 4], F32)

    with ExitStack() as ctx:
      try:
        E = ctx.enter_context
        te, sy, gp = nc.tensor, nc.sync, nc.gpsimd

        ld = Sem(nc, "ld")
        gl = Sem(nc, "gl")
        sv = Sem(nc, "sv")
        ss = Sem(nc, "ss")
        sp = Sem(nc, "sp")
        so = Sem(nc, "so")
        gla = Sem(nc, "gla")
        glb = Sem(nc, "glb")
        gsl = [gla, glb]
        ve = SyncEng(nc.vector, sv)
        se = SyncEng(nc.scalar, ss)

        sq_scr = E(nc.sbuf_tensor("sq_scr", [128, D], F32))
        rp_scr = E(nc.sbuf_tensor("rp_scr", [128, 128], F32))
        stat = E(nc.sbuf_tensor("stat", [128, 8], F32))
        ident_bf = E(nc.sbuf_tensor("ident_bf", [128, 128], BF16))
        ones_col = E(nc.sbuf_tensor("ones_col", [128, 1], BF16))
        ones_row = E(nc.sbuf_tensor("ones_row", [1, 128], BF16))
        biasM = E(nc.sbuf_tensor("biasM", [128, H], F32))
        mxabs = E(nc.sbuf_tensor("mxabs", [128, H], F32))
        small_bf = E(nc.sbuf_tensor("small_bf", [128, 128], BF16))
        eps_ln = E(nc.sbuf_tensor("eps_ln", [128, 1], F32))
        gwbuf = E(nc.sbuf_tensor("gwbuf", [128, DT * NE], BF16))
        eps_l2 = E(nc.sbuf_tensor("eps_l2", [128, 1], F32))
        PS = E(nc.psum_tensor("PS", [128, 4096], F32))
        psb = lambda i, w=512: PS[:, i * 512:i * 512 + w]
        PSbf = PS.ap().bitcast(BF16)

        def gdma(**kw):
            gp.wait_ge(gl.h, gl.n)
            gp.wait_ge(sp.h, sp.n)
            return gl.inc(gp.dma_start(**kw), 16)

        gdma(out=ident_bf[:], in_=ident_in)
        gdma(out=gwbuf.ap().rearrange("p (k n) -> p k n", k=DT),
             in_=gatew_in.rearrange("(k p) n -> p k n", p=128))
        const_gl = gl.n
        ve.memset(ones_col[:], 1.0)
        ve.memset(eps_ln[:], EPS_LN)
        ve.memset(eps_l2[:], EPS_L2)
        ve.memset(ones_row[:], 1.0)
        w_mx0 = sv.inc(ve.memset(mxabs[:, 0:H], 0.0))
        te.wait_ge(gl.h, const_gl)
        te.wait_ge(sv.h, w_mx0)
        se.wait_ge(sv.h, w_mx0)

        def tile_barrier():
            c_sv, c_ss, c_sp, c_so, c_ld, c_gl = sv.n, ss.n, sp.n, so.n, ld.n, gl.n
            nc.vector.wait_ge(ss.h, c_ss)
            nc.vector.wait_ge(sp.h, c_sp)
            nc.vector.wait_ge(so.h, c_so)
            nc.scalar.wait_ge(sv.h, c_sv)
            nc.scalar.wait_ge(sp.h, c_sp)
            nc.scalar.wait_ge(so.h, c_so)
            te.wait_ge(sv.h, c_sv)
            te.wait_ge(ss.h, c_ss)
            sy.wait_ge(sv.h, c_sv)
            sy.wait_ge(ss.h, c_ss)
            sy.wait_ge(sp.h, c_sp)
            gp.wait_ge(sp.h, c_sp)

        def ln_tile(xin, yout):
            s1 = stat[:, 0:1]; s2 = stat[:, 1:2]; m = stat[:, 2:3]
            var = stat[:, 3:4]; rstd = stat[:, 4:5]
            ve.tensor_reduce(s1, xin, axis=AX.X, op=ALU.add)
            w_s = ss.inc(se.activation(sq_scr[:, 0:D], xin, AF.Square, bias=0.0,
                                       scale=1.0, accum_out=s2))
            ve.wait_ge(ss.h, w_s)
            ve.tensor_scalar_mul(m, s1, 1.0 / D)
            ve.tensor_scalar_mul(var, s2, 1.0 / D)
            ve.tensor_tensor(s1, m, m, ALU.mult)
            w_v = sv.inc(ve.tensor_tensor(var, var, s1, ALU.subtract))
            se.wait_ge(sv.h, w_v)
            w_s2 = ss.inc(se.activation(stat[:, 6:7], var, AF.Sqrt, bias=eps_ln[:, 0:1], scale=1.0))
            ve.wait_ge(ss.h, w_s2)
            ve.reciprocal(rstd, stat[:, 6:7])
            return sv.inc(ve.tensor_scalar(yout, xin, m, rstd, ALU.subtract, ALU.mult))

        def transposes(src_bf, col_tiles, dst, dst_col0, pbank0):
            nt_ = len(col_tiles)
            te.wait_ge(sv.h, sv.n)
            te.wait_ge(ss.h, ss.n)
            nc.vector.wait_ge(so.h, so.n)
            done = 0
            while done < nt_:
                grp = min(8, nt_ - done)
                bank = pbank0 + (done // 8) % 2
                for i in range(grp):
                    c = col_tiles[done + i]
                    last = te.transpose(
                        PSbf[:, bank * 1024 + i * 128: bank * 1024 + (i + 1) * 128],
                        src_bf[:, c * 128:(c + 1) * 128], ident_bf[:])
                w_t = sp.inc(last)
                ve.wait_ge(sp.h, w_t)
                w_v = sv.inc(ve.tensor_copy(
                    dst[:, (dst_col0 + done) * 128: (dst_col0 + done + grp) * 128],
                    PSbf[:, bank * 1024: bank * 1024 + grp * 128]))
                te.wait_ge(sv.h, w_v)
                done += grp
            return sv.n

        def rope_tile(src, dst, cosap, sinap, t1buf, t2buf):
            HQ = H * 64
            v3 = lambda ap: ap.rearrange("p (h q d) -> p h q d", h=H, q=2)
            X1 = v3(src)[:, :, 0, :]
            X2 = v3(src)[:, :, 1, :]
            O1 = v3(dst)[:, :, 0, :]
            O2 = v3(dst)[:, :, 1, :]
            c3 = lambda flat: flat.rearrange("p (h d) -> p h d", h=H)
            COS = c3(cosap); SIN = c3(sinap)
            T1 = c3(t1buf.ap()[:, 0:HQ])
            T2 = c3(t2buf.ap()[:, 0:HQ])
            ve.tensor_tensor(T1, X1, COS, ALU.mult)
            ve.tensor_tensor(T2, X2, SIN, ALU.mult)
            ve.tensor_tensor(O1, T1, T2, ALU.subtract)
            ve.tensor_tensor(T1, X1, SIN, ALU.mult)
            ve.tensor_tensor(T2, X2, COS, ALU.mult)
            return sv.inc(ve.tensor_tensor(O2, T1, T2, ALU.add))

        with ExitStack() as px:
            Ex = px.enter_context
            wbuf = Ex(nc.sbuf_tensor("wbuf", [128, DT * D], BF16))
            wbbuf = Ex(nc.sbuf_tensor("wbbuf", [128, DT * H], BF16))
            xbuf_ = Ex(nc.sbuf_tensor("xbuf", [128, 2 * D], F32))
            xbuf = lambda i: xbuf_[:, (i % 2) * D:(i % 2) * D + D]
            ybuf = Ex(nc.sbuf_tensor("ybuf", [128, D], BF16))
            ydm = Ex(nc.sbuf_tensor("ydm", [128, D], BF16))
            prj_ = Ex(nc.sbuf_tensor("prj", [128, 2 * D], F32))
            prj = lambda i: prj_[:, (i % 2) * D:(i % 2) * D + D]
            rop = Ex(nc.sbuf_tensor("rop", [128, D], F32))
            ropb = Ex(nc.sbuf_tensor("ropb", [128, D], BF16))
            dmb = Ex(nc.sbuf_tensor("dmb", [128, D], BF16))
            ktb = Ex(nc.sbuf_tensor("ktb", [128, D], BF16))
            HQ = H * 64
            ctab_ = Ex(nc.sbuf_tensor("ctab", [128, 4 * HQ], F32))
            ctab2 = lambda p, t: ctab_[:, (p * 2 + t) * HQ:(p * 2 + t + 1) * HQ]
            rp1_ = Ex(nc.sbuf_tensor("rp1", [128, HQ], F32))
            rp2_ = Ex(nc.sbuf_tensor("rp2", [128, HQ], F32))
            l2s = Ex(nc.sbuf_tensor("l2s", [128, D], F32))
            beta2_ = Ex(nc.sbuf_tensor("beta2", [128, 2 * H], F32))
            beta2 = lambda i: beta2_[:, (i % 2) * H:(i % 2) * H + H]
            lstat = Ex(nc.sbuf_tensor("lstat", [128, H], F32))

            def load_w(dramw, buf):
                return gdma(
                    out=buf.ap().rearrange("p (k n) -> p k n", k=DT),
                    in_=dramw.rearrange("(k p) n -> p k n", p=128))

            def proj_tile(wsrc, out_f32, width, pb0):
                nchunk = width // 512 if width >= 512 else 1
                cw0 = min(512, width)
                last = None
                for c in range(nchunk):
                    for k in range(DT):
                        last = te.matmul(
                            psb(pb0 + c, cw0),
                            ydm[:, k * 128:(k + 1) * 128],
                            wsrc[:, k * width + c * 512: k * width + c * 512 + cw0],
                            start=(k == 0), stop=(k == DT - 1))
                w_t = sp.inc(last)
                ve.wait_ge(sp.h, w_t)
                lastv = None
                for c in range(nchunk):
                    lastv = ve.tensor_copy(out_f32[:, c * 512:c * 512 + cw0], psb(pb0 + c, cw0))
                w_v = sv.inc(lastv)
                te.wait_ge(sv.h, w_v)
                return w_v

            xfree_v = [0, 0]; xfree_s = [0, 0]
            wx = [0, 0]; wct = [0, 0]
            ctab_free = [0, 0]
            st8 = dict(tpin=0, tpout=0, tpout2=0, sg=0, ktb=0, sq=0, kspill=0)

            def qk_loads(jt, p, srcdram, costab, sintab):
                sy.wait_ge(sv.h, xfree_v[p])
                sy.wait_ge(ss.h, xfree_s[p])
                wx[p] = ld.inc(sy.dma_start(out=xbuf(p)[:],
                                            in_=srcdram[jt * 128:(jt + 1) * 128, :]), 16)
                sy.wait_ge(sv.h, ctab_free[p])
                ld.inc(sy.dma_start(out=ctab2(p, 0),
                                    in_=costab[jt * 128:(jt + 1) * 128, :]), 16)
                wct[p] = ld.inc(sy.dma_start(out=ctab2(p, 1),
                                             in_=sintab[jt * 128:(jt + 1) * 128, :]), 16)

            sqdone = [0, 0]
            b2free = [0, 0]
            asg = {}
            aprj = {}

            def phase_A(jt, is_k, wgl_done):
                p = jt % 2
                nc.vector.wait_ge(ld.h, wx[p])
                nc.scalar.wait_ge(ld.h, wx[p])
                nc.vector.wait_ge(sp.h, st8["tpin"])
                ln_tile(xbuf(p)[:], ybuf[:])
                xfree_v[p] = sv.n
                xfree_s[p] = ss.n
                w_tpin = transposes(ybuf, list(range(DT)), ydm, 0, 0)
                st8["tpin"] = sp.n
                if is_k:
                    sy.wait_ge(sv.h, w_tpin)
                    so.inc(sy.dma_start(
                        out=ydm_sp[jt].rearrange("t p n -> p t n"),
                        in_=ydm.ap().rearrange("p (t n) -> p t n", t=DT)), 16)
                if jt == 0:
                    te.wait_ge(gl.h, wgl_done)
                if is_k:
                    nc.vector.wait_ge(ss.h, sqdone[p])
                aprj[jt] = proj_tile(wbuf, prj(p), D, 2)
                if is_k:
                    te.wait_ge(ss.h, st8["sg"])
                    lastb = None
                    for k in range(DT):
                        lastb = te.matmul(psb(6, H), ydm[:, k * 128:(k + 1) * 128],
                                          wbbuf[:, k * H:(k + 1) * H],
                                          start=(k == 0), stop=(k == DT - 1))
                    w_bt = sp.inc(lastb)
                    se.wait_ge(sp.h, w_bt)
                    se.wait_ge(sv.h, b2free[p])
                    w_sg = ss.inc(se.activation(beta2(p)[:], psb(6, H), AF.Sigmoid,
                                                bias=0.0, scale=1.0))
                    st8["sg"] = w_sg
                    asg[jt] = w_sg

            def phase_B(jt, is_k, n_all, srcdram, costab, sintab, spill_dst):
                p = jt % 2
                if is_k:
                    se.wait_ge(sv.h, st8["l2red"])
                    se.wait_ge(sv.h, aprj[jt])
                    w_sq = ss.inc(se.activation(l2s[:, 0:D], prj(p)[:], AF.Square,
                                                bias=0.0, scale=1.0))
                    sqdone[p] = w_sq
                    ve.wait_ge(ss.h, w_sq)
                    r3 = l2s.ap().rearrange("p (h d) -> p h d", h=H)
                    w_red = sv.inc(ve.tensor_reduce(lstat[:, 0:H], r3, axis=AX.X,
                                                    op=ALU.add))
                    st8["l2red"] = w_red
                    se.wait_ge(sv.h, w_red)
                    w_sqr = ss.inc(se.activation(lstat[:, 0:H], lstat[:, 0:H], AF.Sqrt,
                                                 bias=eps_l2[:, 0:1], scale=1.0))
                    ve.wait_ge(ss.h, w_sqr)
                    ve.reciprocal(lstat[:, 0:H], lstat[:, 0:H])
                    for h in range(H):
                        ve.tensor_scalar_mul(prj(p)[:, h * 128:(h + 1) * 128],
                                             prj(p)[:, h * 128:(h + 1) * 128],
                                             lstat[:, h:h + 1])
                nc.vector.wait_ge(ld.h, wct[p])
                w_rope = rope_tile(prj(p), rop, ctab2(p, 0), ctab2(p, 1), rp1_, rp2_)
                ctab_free[p] = w_rope
                if not is_k:
                    r3 = rop.ap().rearrange("p (h d) -> p h d", h=H)
                    ve.tensor_reduce(lstat[:, 0:H], r3, axis=AX.X, op=ALU.max,
                                     apply_absolute_value=True)
                    ve.tensor_tensor(mxabs[:, 0:H], mxabs[:, 0:H], lstat[:, 0:H],
                                     ALU.max)
                if is_k:
                    ve.wait_ge(ss.h, asg[jt])
                    ve.tensor_scalar_mul(beta2(p)[:], beta2(p)[:], 2.0)
                    nc.vector.wait_ge(sp.h, st8["tpout2"])
                    nc.vector.wait_ge(so.h, st8["kspill"])
                    lastk = None
                    for h in range(H):
                        lastk = ve.tensor_scalar_mul(ktb[:, h * 128:(h + 1) * 128],
                                                     rop[:, h * 128:(h + 1) * 128],
                                                     beta2(p)[:, h:h + 1])
                    b2free[p] = lastk
                nc.vector.wait_ge(sp.h, st8["tpout"])
                w_cast = sv.inc(ve.tensor_copy(ropb[:], rop[:]))
                w_tp = transposes(ropb, list(range(DT)), dmb, 0, 6)
                st8["tpout"] = sp.n
                sy.wait_ge(sv.h, w_tp)
                so.inc(sy.dma_start(
                    out=spill_dst[jt].rearrange("t p n -> p t n"),
                    in_=dmb.ap().rearrange("p (t n) -> p t n", t=DT)), 16)
                if is_k:
                    w_tp2 = transposes(ktb, list(range(H)), dmb, 0, 6)
                    st8["tpout2"] = sp.n
                    sy.wait_ge(sv.h, w_tp2)
                    so.inc(sy.dma_start(
                        out=ktdm_sp[jt].rearrange("t p n -> p t n"),
                        in_=dmb.ap().rearrange("p (t n) -> p t n", t=H)), 16)
                    so.inc(sy.dma_start(out=kttm_sp[jt], in_=ktb[:]), 16)
                    st8["kspill"] = so.n
                if jt + 2 < n_all:
                    qk_loads(jt + 2, p, srcdram, costab, sintab)

            st8["l2red"] = 0

            wq_done = load_w(wq_in, wbuf)
            qk_loads(0, 0, x_q, cos_q, sin_q)
            qk_loads(1, 1, x_q, cos_q, sin_q)
            phase_A(0, False, wq_done)
            for jt in range(NQT):
                if jt + 1 < NQT:
                    phase_A(jt + 1, False, wq_done)
                phase_B(jt, False, NQT, x_q, cos_q, sin_q, q_sp)

            w_mxc = sv.inc(ve.tensor_copy(small_bf[:, 0:H], mxabs[:, 0:H]))
            te.wait_ge(sv.h, w_mxc)
            w_t1 = sp.inc(te.transpose(PSbf[0:H, 0:128], small_bf[:, 0:H], ident_bf[:]))
            ve.wait_ge(sp.h, w_t1)
            ve.tensor_reduce(stat[0:H, 0:1], PSbf[0:H, 0:128], axis=AX.X, op=ALU.max)
            ve.tensor_scalar_mul(stat[0:H, 1:2], stat[0:H, 0:1], -2.0)
            w_mxs = sv.inc(ve.tensor_copy(small_bf[0:H, 0:1], stat[0:H, 1:2]))
            te.wait_ge(sv.h, w_mxs)
            w_t2 = sp.inc(te.transpose(PSbf[0:1, 0:H], small_bf[0:H, 0:1], ident_bf[0:H, 0:H]))
            ve.wait_ge(sp.h, w_t2)
            w_mr = sv.inc(ve.tensor_copy(small_bf[0:1, 0:H], PSbf[0:1, 0:H]))
            te.wait_ge(sv.h, w_mr)
            w_bc = sp.inc(te.matmul(psb(1, H), ones_row[:], small_bf[0:1, 0:H],
                                    start=True, stop=True))
            se.wait_ge(sp.h, w_bc)
            w_bM = ss.inc(se.copy(biasM[:, 0:H], psb(1, H)))

            wk_done = load_w(wk_in, wbuf)
            wb_done = gdma(out=wbbuf.ap().rearrange("p (k n) -> p k n", k=DT),
                           in_=wb_in.rearrange("(k p) n -> p k n", p=128))
            qk_loads(0, 0, x_full, cos_kv, sin_kv)
            qk_loads(1, 1, x_full, cos_kv, sin_kv)
            phase_A(0, True, wb_done)
            for kt in range(NT):
                if kt + 1 < NT:
                    phase_A(kt + 1, True, wb_done)
                phase_B(kt, True, NT, x_full, cos_kv, sin_kv, kdm_sp)
            sy.wait_ge(so.h, so.n)

            wv_done = load_w(wv_in, wbuf)
            tile_barrier()
            ydms = [ydm, dmb]
            vouts = [ropb, ktb]
            v_proj_done = [0, 0]
            v_cast_done = [0, 0]
            v_spill_done = [so.n, so.n]
            v_wl = [0, 0]

            def v_load(kt):
                p = kt % 2
                sy.wait_ge(sp.h, v_proj_done[p])
                v_wl[p] = ld.inc(sy.dma_start(
                    out=ydms[p].ap().rearrange("p (t n) -> p t n", t=DT),
                    in_=ydm_sp[kt].rearrange("t p n -> p t n")), 16)

            v_load(0)
            for kt in range(NT):
                p = kt % 2
                if kt + 1 < NT:
                    v_load(kt + 1)
                te.wait_ge(ld.h, v_wl[p])
                if kt == 0:
                    te.wait_ge(gl.h, wv_done)
                te.wait_ge(sv.h, v_cast_done[p])
                last = None
                for c in range(4):
                    for k in range(DT):
                        last = te.matmul(psb(4 * p + c, 512),
                                         ydms[p][:, k * 128:(k + 1) * 128],
                                         wbuf[:, k * D + c * 512: k * D + c * 512 + 512],
                                         start=(k == 0), stop=(k == DT - 1))
                w_t = sp.inc(last)
                v_proj_done[p] = w_t
                ve.wait_ge(sp.h, w_t)
                ve.wait_ge(so.h, v_spill_done[p])
                lastv = None
                for c in range(4):
                    lastv = ve.tensor_copy(vouts[p][:, c * 512:(c + 1) * 512],
                                           psb(4 * p + c, 512))
                w_cst = sv.inc(lastv)
                v_cast_done[p] = w_cst
                sy.wait_ge(sv.h, w_cst)
                v_spill_done[p] = so.inc(sy.dma_start(out=vtm_sp[kt], in_=vouts[p][:]), 16)
            sy.wait_ge(so.h, so.n)

        if stop_phase == "qkv":
            raise_stop = True
        if stop_phase == "qkv":
            raise _StopBuild
        with ExitStack() as px:
            Ex = px.enter_context
            tmasks_bf = Ex(nc.sbuf_tensor("tmasks_bf", [128, 7 * HB], BF16))
            irep_bf = Ex(nc.sbuf_tensor("irep_bf", [128, HB], BF16))
            tworep_bf = Ex(nc.sbuf_tensor("tworep_bf", [128, HB], BF16))
            kdm_c_ = Ex(nc.sbuf_tensor("kdm_c", [128, 2 * HB], BF16))
            kdm_c = lambda i: kdm_c_[:, i * HB:(i + 1) * HB]
            ktdm_c_ = Ex(nc.sbuf_tensor("ktdm_c", [128, 2 * HB], BF16))
            ktdm_c = lambda i: ktdm_c_[:, i * HB:(i + 1) * HB]
            kttm_c_ = Ex(nc.sbuf_tensor("kttm_c", [128, 2 * D], BF16))
            kttm_c = lambda i: kttm_c_[:, i * D:(i + 1) * D]
            vtm_c_ = Ex(nc.sbuf_tensor("vtm_c", [128, 2 * D], BF16))
            vtm_c = lambda i: vtm_c_[:, i * D:(i + 1) * D]
            Gsb = Ex(nc.sbuf_tensor("Gsb", [128, HB], BF16))
            Gm = Ex(nc.sbuf_tensor("Gm", [128, 7 * HB], BF16))
            Tac_ = Ex(nc.sbuf_tensor("Tac", [128, 2 * HB], BF16))
            Tac = lambda i: Tac_[:, i * HB:(i + 1) * HB]
            TacT_ = Ex(nc.sbuf_tensor("TacT", [128, 2 * HB], BF16))
            TacT = lambda i: TacT_[:, i * HB:(i + 1) * HB]
            Vb = Ex(nc.sbuf_tensor("Vb", [128, HB], BF16))
            VbT = Ex(nc.sbuf_tensor("VbT", [128, HB], BF16))
            rhs0 = Ex(nc.sbuf_tensor("rhs0", [128, HB], BF16))
            vnb = Ex(nc.sbuf_tensor("vnb", [128, HB], BF16))
            S_f = Ex(nc.sbuf_tensor("S_f", [128, HB], F32))
            S_b = Ex(nc.sbuf_tensor("S_b", [128, HB], BF16))

            gdma(out=tmasks_bf.ap().rearrange("p (k n) -> p k n", k=7), in_=tmasks_in.rearrange("k p n -> p k n"))
            gdma(out=irep_bf[:], in_=irep_in)
            scan_gl = gdma(out=tworep_bf[:], in_=tworep_in)
            tmask = lambda k: tmasks_bf[:, k * HB:(k + 1) * HB]
            hs = lambda buf, h: buf[:, h * 128:(h + 1) * 128]
            PSlo = PS[:, 0:HB]
            PShi = PS[:, 2048:2048 + HB]

            w_i1 = sv.inc(ve.memset(S_f[:], 0.0))
            se.wait_ge(sv.h, w_i1)
            w_i2 = ss.inc(se.copy(S_b[:], S_f[:]))
            te.wait_ge(ss.h, w_i2)
            te.wait_ge(gl.h, scan_gl)

            chunk_te_done = [0, 0]
            for kt in range(NT):
                tile_barrier()
                pb = kt % 2
                ld.inc(sy.dma_start(
                    out=kdm_c(pb).rearrange("p (t n) -> p t n", t=H), in_=kdm_sp[kt].rearrange("t p n -> p t n")), 16)
                ld.inc(sy.dma_start(
                    out=ktdm_c(pb).rearrange("p (t n) -> p t n", t=H), in_=ktdm_sp[kt].rearrange("t p n -> p t n")), 16)
                ld.inc(sy.dma_start(out=kttm_c(pb), in_=kttm_sp[kt]), 16)
                wl = ld.inc(sy.dma_start(out=vtm_c(pb), in_=vtm_sp[kt]), 16)
                te.wait_ge(ld.h, wl)
                ve.wait_ge(ld.h, wl)
                lastA = None
                for h in range(H):
                    b0, off = divmod(h * 128, 512)
                    lastA = te.matmul(psb(b0)[:, off:off + 128], hs(kdm_c(pb), h),
                                      hs(ktdm_c(pb), h), start=True, stop=True)
                w_A = sp.inc(lastA)
                ve.wait_ge(sp.h, w_A)
                w_G = sv.inc(ve.tensor_copy(Gsb[:], PSlo))
                lastm = None
                for k in range(7):
                    lastm = ve.tensor_tensor(Gm[:, k * HB:(k + 1) * HB], Gsb[:],
                                             tmask(k), ALU.mult)
                w_Gm = sv.inc(lastm)
                te.wait_ge(sv.h, w_Gm)
                sv.inc(ve.scalar_tensor_tensor(Vb[:], Gm[:, 0:HB], -1.0,
                                               irep_bf[:], ALU.mult, ALU.add))
                last0 = None
                for h in range(H):
                    b0, off = divmod(h * 128, 512)
                    last0 = te.matmul(psb(b0)[:, off:off + 128],
                                      Gm[:, h * 128:(h + 1) * 128],
                                      ident_bf[:], start=True, stop=True)
                w_l0 = sp.inc(last0)
                ve.wait_ge(sp.h, w_l0)
                w_v0t = sv.inc(ve.scalar_tensor_tensor(VbT[:], PSlo, -1.0,
                                                       irep_bf[:], ALU.mult, ALU.add))
                ve.tensor_copy(Tac(0), Vb[:])
                w_tc = sv.inc(ve.tensor_copy(TacT(0), VbT[:]))
                te.wait_ge(sv.h, w_tc)
                cur = 0
                for lv in range(1, 7):
                    gmk = Gm[:, lv * HB:(lv + 1) * HB]
                    lastD = None
                    for h in range(H):
                        b0, off = divmod(h * 128, 512)
                        te.matmul(psb(b0)[:, off:off + 128], hs(TacT(cur), h),
                                  gmk[:, h * 128:(h + 1) * 128], start=True, stop=False)
                        te.matmul(psb(b0)[:, off:off + 128], hs(TacT(cur), h),
                                  ident_bf[:], start=False, stop=True)
                        te.matmul(psb(4 + b0)[:, off:off + 128],
                                  gmk[:, h * 128:(h + 1) * 128],
                                  hs(TacT(cur), h), start=True, stop=False)
                        lastD = te.matmul(psb(4 + b0)[:, off:off + 128], ident_bf[:],
                                          hs(TacT(cur), h), start=False, stop=True)
                    w_D = sp.inc(lastD)
                    ve.wait_ge(sp.h, w_D)
                    se.wait_ge(sp.h, w_D)
                    w_V = sv.inc(ve.scalar_tensor_tensor(Vb[:], PSlo, -1.0,
                                                         tworep_bf[:], ALU.mult, ALU.add))
                    w_nT = ss.inc(se.activation(VbT[:], PShi, AF.Copy, bias=0.0, scale=-1.0))
                    ve.wait_ge(ss.h, w_nT)
                    w_VT = sv.inc(ve.tensor_tensor(VbT[:], VbT[:], tworep_bf[:], ALU.add))
                    te.wait_ge(sv.h, w_VT)
                    nxt = 1 - cur
                    lastT = None
                    for h in range(H):
                        b0, off = divmod(h * 128, 512)
                        te.matmul(psb(b0)[:, off:off + 128], hs(VbT, h), hs(Tac(cur), h),
                                  start=True, stop=True)
                        lastT = te.matmul(psb(4 + b0)[:, off:off + 128], hs(Tac(cur), h),
                                          hs(VbT, h), start=True, stop=True)
                    w_T = sp.inc(lastT)
                    ve.wait_ge(sp.h, w_T)
                    se.wait_ge(sp.h, w_T)
                    w_e1 = sv.inc(ve.tensor_copy(Tac(nxt), PSlo))
                    w_e2 = ss.inc(se.copy(TacT(nxt), PShi))
                    te.wait_ge(sv.h, w_e1)
                    te.wait_ge(ss.h, w_e2)
                    cur = nxt
                lastR = None
                for h in range(H):
                    b0, off = divmod(h * 128, 512)
                    lastR = te.matmul(psb(b0)[:, off:off + 128], hs(kdm_c(pb), h),
                                      hs(S_b, h), start=True, stop=True)
                w_R = sp.inc(lastR)
                ve.wait_ge(sp.h, w_R)
                w_r0 = sv.inc(ve.scalar_tensor_tensor(rhs0[:], PSlo, -1.0,
                                                      vtm_c(pb)[:, 0:HB], ALU.mult, ALU.add))
                te.wait_ge(sv.h, w_r0)
                lastV = None
                for h in range(H):
                    b0, off = divmod(h * 128, 512)
                    lastV = te.matmul(psb(4 + b0)[:, off:off + 128], hs(TacT(cur), h),
                                      hs(rhs0, h), start=True, stop=True)
                w_Vn = sp.inc(lastV)
                ve.wait_ge(sp.h, w_Vn)
                sy.wait_ge(so.h, so.n)
                w_vnb = sv.inc(ve.tensor_copy(vnb[:], PShi))
                te.wait_ge(sv.h, w_vnb)
                sy.wait_ge(sv.h, w_vnb)
                so.inc(sy.dma_start(
                    out=vn_sp[kt].rearrange("t p n -> p t n"), in_=vnb.ap().rearrange("p (t n) -> p t n", t=H)), 16)
                lastS = None
                for h in range(H):
                    b0, off = divmod(h * 128, 512)
                    lastS = te.matmul(psb(b0)[:, off:off + 128], hs(kttm_c(pb), h),
                                      hs(vnb, h), start=True, stop=True)
                w_S = sp.inc(lastS)
                ve.wait_ge(sp.h, w_S)
                w_Sa = sv.inc(ve.tensor_tensor(S_f[:], S_f[:], PSlo, ALU.add))
                se.wait_ge(sv.h, w_Sa)
                w_Sb = ss.inc(se.copy(S_b[:], S_f[:]))
                te.wait_ge(ss.h, w_Sb)
                chunk_te_done[pb] = sp.n
            sy.wait_ge(so.h, so.n)

        if stop_phase == "scan":
            raise _StopBuild
        with ExitStack() as po:
            Eo = po.enter_context
            outT = Eo(nc.sbuf_tensor("outT", [128, H * NQ], BF16))
            with ExitStack() as px:
                Ex = px.enter_context
                kth_ = Ex(nc.sbuf_tensor("kth", [128, 2 * N], BF16))
                kth = lambda s: kth_[:, s * N:(s + 1) * N]
                vnh_ = Ex(nc.sbuf_tensor("vnh", [128, 2 * N], BF16))
                vnh = lambda s: vnh_[:, s * N:(s + 1) * N]
                qh_ = Ex(nc.sbuf_tensor("qh", [128, 2 * NQ], BF16))
                qh = lambda s: qh_[:, s * NQ:(s + 1) * NQ]
                expb_ = Ex(nc.sbuf_tensor("expb", [128, 512], BF16))
                expb = lambda i: expb_[:, i * 256:(i + 1) * 256]
                expm_ = Ex(nc.sbuf_tensor("expm", [128, 512], BF16))
                expm = lambda i: expm_[:, i * 256:(i + 1) * 256]
                denf = Ex(nc.sbuf_tensor("denf", [1, 512], F32))
                denb = Ex(nc.sbuf_tensor("denb", [1, 512], BF16))
                dbc = Ex(nc.sbuf_tensor("dbc", [128, 512], BF16))
                pal_bf = Ex(nc.sbuf_tensor("pal_bf", [128, 4 * 256], BF16))
                pal_gl = gdma(out=pal_bf.ap().rearrange("p (k n) -> p k n", k=4),
                              in_=pal_in.rearrange("k p n -> p k n"))
                nc.vector.wait_ge(gl.h, pal_gl)

                att_done = [0, 0]
                av_free = [0, 0]; den_free = [0, 0]; dbc_free = [0, 0]
                bc_free = [0, 0]; db_free = [0, 0]

                def head_loads(h):
                    s = h % 2
                    sy.wait_ge(sp.h, att_done[s])
                    ld.inc(sy.dma_start(
                        out=kth(s).rearrange("p (t n) -> p t n", t=NT),
                        in_=ktdm_sp[:, h].rearrange("t p n -> p t n")), 16)
                    ld.inc(sy.dma_start(
                        out=vnh(s).rearrange("p (t n) -> p t n", t=NT),
                        in_=vn_sp[:, h].rearrange("t p n -> p t n")), 16)
                    return ld.inc(sy.dma_start(
                        out=qh(s).rearrange("p (t n) -> p t n", t=NQT),
                        in_=q_sp[:, h].rearrange("t p n -> p t n")), 16)

                def den_reg(mp):
                    return PS[0:1, 4 * 512 + mp * 256: 4 * 512 + mp * 256 + 256]

                def bc_reg(mp):
                    return PS[:, 5 * 512 + mp * 256: 5 * 512 + mp * 256 + 256]

                def den_head(mm, w_pv):
                    mp = mm % 2
                    se.wait_ge(sp.h, w_pv)
                    se.wait_ge(sv.h, db_free[mp])
                    w_dn = ss.inc(se.copy(denf[:, mp * 256:mp * 256 + 256], den_reg(mp)))
                    den_free[mp] = w_dn
                    ve.wait_ge(ss.h, w_dn)
                    ve.reciprocal(denf[:, mp * 256:mp * 256 + 256],
                                  denf[:, mp * 256:mp * 256 + 256])
                    w_db = sv.inc(ve.tensor_copy(denb[:, mp * 256:mp * 256 + 256],
                                                 denf[:, mp * 256:mp * 256 + 256]))
                    db_free[mp] = w_db
                    return w_db

                def emit_chain(hh, mm, w_db):
                    mp = mm % 2
                    te.wait_ge(sv.h, w_db)
                    te.wait_ge(ss.h, bc_free[mp])
                    w_bc2 = sp.inc(te.matmul(bc_reg(mp), ones_row[:],
                                             denb[:, mp * 256:mp * 256 + 256],
                                             start=True, stop=True))
                    se.wait_ge(sp.h, w_bc2)
                    se.wait_ge(sv.h, dbc_free[mp])
                    w_dbc = ss.inc(se.copy(dbc[:, mp * 256:mp * 256 + 256], bc_reg(mp)))
                    bc_free[mp] = w_dbc
                    ve.wait_ge(ss.h, w_dbc)
                    w_o = sv.inc(ve.tensor_tensor(
                        outT[:, hh * NQ + mm * 256: hh * NQ + (mm + 1) * 256],
                        psb(2 + mp, 256), dbc[:, mp * 256:mp * 256 + 256], ALU.mult))
                    av_free[mp] = w_o
                    dbc_free[mp] = w_o

                sy.wait_ge(sv.h, sv.n)
                sy.wait_ge(ss.h, ss.n)
                sy.wait_ge(sp.h, sp.n)
                wlh = [0, 0]
                wlh[0] = head_loads(0)
                pend = None
                for h in range(H):
                    s = h % 2
                    te.wait_ge(ld.h, wlh[s])
                    se.wait_ge(ld.h, wlh[s])
                    if h + 1 < H:
                        wlh[1 - s] = head_loads(h + 1)
                    for m in range(QC):
                        mp = m % 2
                        nkt = 4 * m + 4
                        qrhs = qh(s)[:, m * 256:(m + 1) * 256]
                        exp_done = [0, 0]
                        w_scores = {}
                        w_scores[0] = sp.inc(te.matmul(psb(0, 256), kth(s)[:, 0:128],
                                                       qrhs, start=True, stop=True))
                        for kt in range(nkt):
                            png = kt % 2
                            if kt + 1 < nkt:
                                te.wait_ge(ss.h, exp_done[1 - png])
                                w_scores[kt + 1] = sp.inc(te.matmul(
                                    psb(1 - png, 256),
                                    kth(s)[:, (kt + 1) * 128:(kt + 2) * 128],
                                    qrhs, start=True, stop=True))
                            se.wait_ge(sp.h, w_scores[kt])
                            w_ex = ss.inc(se.activation(expb(png), psb(png, 256), AF.Exp,
                                                        bias=biasM[:, h:h + 1], scale=SCALE))
                            exp_done[png] = w_ex
                            c = kt - 4 * m
                            if c >= 0:
                                ve.wait_ge(ss.h, w_ex)
                                w_mk = sv.inc(ve.tensor_tensor(
                                    expm(png), expb(png),
                                    pal_bf[:, c * 256:(c + 1) * 256], ALU.mult))
                                te.wait_ge(sv.h, w_mk)
                                src = expm(png)
                            else:
                                te.wait_ge(ss.h, w_ex)
                                src = expb(png)
                            if kt == 0:
                                te.wait_ge(sv.h, av_free[mp])
                                te.wait_ge(ss.h, den_free[mp])
                            te.matmul(psb(2 + mp, 256), vnh(s)[:, kt * 128:(kt + 1) * 128],
                                      src[:], start=(kt == 0), stop=(kt == nkt - 1))
                            lastP = te.matmul(den_reg(mp), ones_col[:],
                                              src[:], start=(kt == 0), stop=(kt == nkt - 1))
                        w_pv = sp.inc(lastP)
                        w_db_m = den_head(m, w_pv)
                        if pend is not None:
                            emit_chain(*pend)
                        pend = (h, m, w_db_m)
                    att_done[s] = w_pv
                emit_chain(*pend)
                pend = None

            if stop_phase == "att":
                raise _StopBuild
            with ExitStack() as px:
                Ex = px.enter_context
                wobuf = Ex(nc.sbuf_tensor("wobuf", [128, DT * D], BF16))
                aodm = Ex(nc.sbuf_tensor("aodm", [128, DT * 1024], BF16))
                aotm = Ex(nc.sbuf_tensor("aotm", [128, D], BF16))
                xq2 = Ex(nc.sbuf_tensor("xq2", [128, D], F32))
                wo_done = gdma(
                    out=wobuf.ap().rearrange("p (k n) -> p k n", k=DT),
                    in_=wo_in.rearrange("(k p) n -> p k n", p=128))
                te.wait_ge(gl.h, wo_done)
                for th in range(NQ // 1024):
                    tile_barrier()
                    for dt in range(DT):
                        for c in range(2):
                            pbk = c % 2
                            lastW = None
                            for k in range(DT):
                                lastW = te.matmul(
                                    psb(pbk, 512),
                                    wobuf[:, k * D + dt * 128: k * D + dt * 128 + 128],
                                    outT[:, k * NQ + th * 1024 + c * 512:
                                         k * NQ + th * 1024 + c * 512 + 512],
                                    start=(k == 0), stop=(k == DT - 1))
                            w_w = sp.inc(lastW)
                            ve.wait_ge(sp.h, w_w)
                            w_ev = sv.inc(ve.tensor_copy(
                                aodm[:, dt * 1024 + c * 512: dt * 1024 + c * 512 + 512],
                                psb(pbk, 512)))
                            te.wait_ge(sv.h, w_ev)
                    for tt in range(8):
                        tile_barrier()
                        cols = [dt * 8 + tt for dt in range(DT)]
                        transposes(aodm, cols, aotm, 0, 2)
                        jt = th * 8 + tt
                        wxr = ld.inc(sy.dma_start(out=xq2[:],
                                                  in_=x_q[jt * 128:(jt + 1) * 128, :]), 16)
                        se.wait_ge(sv.h, sv.n)
                        ln_tile(aotm[:], sq_scr[:, 0:D])
                        ve.wait_ge(ld.h, wxr)
                        w_x1 = sv.inc(ve.tensor_tensor(xq2[:], xq2[:], sq_scr[:, 0:D],
                                                       ALU.add))
                        sy.wait_ge(sv.h, w_x1)
                        wst = so.inc(sy.dma_start(out=x1_sp[jt * 128:(jt + 1) * 128, :],
                                                  in_=xq2[:]), 16)
                        sy.wait_ge(so.h, wst)

        if stop_phase == "o":
            raise _StopBuild
        with ExitStack() as px:
            Ex = px.enter_context
            NHALF = NQ // 2
            HT = NHALF // 512
            zdm = Ex(nc.sbuf_tensor("zdm", [128, DT * NHALF], BF16))
            x1b = Ex(nc.sbuf_tensor("x1b", [128, D], F32))
            ztm = Ex(nc.sbuf_tensor("ztm", [128, D], BF16))
            hbuf = Ex(nc.sbuf_tensor("hbuf", [128, FT * NHALF], BF16))
            outacc = Ex(nc.sbuf_tensor("outacc", [128, DT * NHALF], F32))
            w1sl_ = Ex(nc.sbuf_tensor("w1sl", [128, 2 * DT * 128], BF16))
            w1sl = lambda i: w1sl_[:, i * DT * 128:(i + 1) * DT * 128]
            wgsl_ = Ex(nc.sbuf_tensor("wgsl", [128, 2 * DT * 128], BF16))
            wgsl = lambda i: wgsl_[:, i * DT * 128:(i + 1) * DT * 128]
            w2sl_ = Ex(nc.sbuf_tensor("w2sl", [128, 2 * FT * 128], BF16))
            w2sl = lambda i: w2sl_[:, i * FT * 128:(i + 1) * FT * 128]
            silub = Ex(nc.sbuf_tensor("silub", [128, 1024], BF16))
            tmulb = Ex(nc.sbuf_tensor("tmulb", [128, 1024], BF16))
            gbb = Ex(nc.sbuf_tensor("gbb", [128, NE], F32))
            bbb = Ex(nc.sbuf_tensor("bbb", [128, NE], F32))
            gsc = Ex(nc.sbuf_tensor("gsc", [128, 4 * NE + 4], F32))
            wtm = Ex(nc.sbuf_tensor("wtm", [128, NE], F32))
            wtmb = Ex(nc.sbuf_tensor("wtmb", [128, NE], BF16))
            wrow = Ex(nc.sbuf_tensor("wrow", [1, NE * NHALF], BF16))
            wbc = Ex(nc.sbuf_tensor("wbc", [128, NHALF], BF16))
            ffntm = Ex(nc.sbuf_tensor("ffntm", [128, D], BF16))

            if dbg:
                sy.wait_ge(gl.h, gl.n)
                wdg0 = so.inc(sy.dma_start(out=dbg_gw0[:, :], in_=gwbuf[:, :]), 16)
                sy.wait_ge(so.h, wdg0)
            ld.inc(sy.dma_start(out=gbb[:, 0:NE], in_=gateb_bc), 16)
            wbb_ld = ld.inc(sy.dma_start(out=bbb[:, 0:NE], in_=biases_bc), 16)

            def load_slice3(dst, dram3, e, colt, ktiles):
                src = dram3[e].rearrange("(k p) m -> p k m", p=128)[:, :, colt * 128:(colt + 1) * 128]
                return gdma(out=dst.rearrange("p (k n) -> p k n", k=ktiles), in_=src)

            def load_slice2(dst, dram2, colt, ktiles):
                src = dram2.rearrange("(k p) m -> p k m", p=128)[:, :, colt * 128:(colt + 1) * 128]
                return gdma(out=dst.rearrange("p (k n) -> p k n", k=ktiles), in_=src)

            slice_te_done = {}
            for hf in range(2):
                for i in range(NHALF // 128):
                    tile_barrier()
                    jt = hf * (NHALF // 128) + i
                    wx1 = ld.inc(sy.dma_start(out=x1b[:],
                                              in_=x1_sp[jt * 128:(jt + 1) * 128, :]), 16)
                    ve.wait_ge(ld.h, wx1)
                    se.wait_ge(ld.h, wx1)
                    ln_tile(x1b[:], ztm[:])
                    te.wait_ge(sv.h, sv.n)
                    done = 0
                    while done < DT:
                        grp = min(8, DT - done)
                        bank = (done // 8) % 2
                        lastT2 = None
                        for g2 in range(grp):
                            dtl = done + g2
                            lastT2 = te.transpose(
                                PSbf[:, bank * 1024 + g2 * 128: bank * 1024 + (g2 + 1) * 128],
                                ztm[:, dtl * 128:(dtl + 1) * 128], ident_bf[:])
                        w_t3 = sp.inc(lastT2)
                        ve.wait_ge(sp.h, w_t3)
                        lastV2 = None
                        for g2 in range(grp):
                            dtl = done + g2
                            lastV2 = ve.tensor_copy(
                                zdm[:, dtl * NHALF + i * 128: dtl * NHALF + (i + 1) * 128],
                                PSbf[:, bank * 1024 + g2 * 128: bank * 1024 + (g2 + 1) * 128])
                        w_v3 = sv.inc(lastV2)
                        te.wait_ge(sv.h, w_v3)
                        done += grp
                    te.wait_ge(gl.h, gl.n)
                    lastG = None
                    for k in range(DT):
                        lastG = te.matmul(psb(2, NE),
                                          zdm[:, k * NHALF + i * 128: k * NHALF + (i + 1) * 128],
                                          gwbuf[:, k * NE:(k + 1) * NE],
                                          start=(k == 0), stop=(k == DT - 1))
                    w_g = sp.inc(lastG)
                    ve.wait_ge(sp.h, w_g)
                    ve.wait_ge(ld.h, ld.n)
                    glog = gsc[:, 0:NE]
                    negmx = gsc[:, NE:NE + 1]
                    gexp = gsc[:, NE + 1:2 * NE + 1]
                    gsum = gsc[:, 2 * NE + 1:2 * NE + 2]
                    probas = gsc[:, 2 * NE + 2:3 * NE + 2]
                    biased = gsc[:, 3 * NE + 2:4 * NE + 2]
                    cnt = gsc[:, 4 * NE + 2:4 * NE + 3]
                    wsum = gsc[:, 4 * NE + 3:4 * NE + 4]
                    ve.tensor_tensor(glog, psb(2, NE), gbb[:, 0:NE], ALU.add)
                    w_gv = sv.inc(ve.tensor_reduce(negmx, glog, axis=AX.X, op=ALU.max,
                                                   negate=True))
                    te.wait_ge(sv.h, w_gv)
                    se.wait_ge(sv.h, w_gv)
                    w_ge = ss.inc(se.activation(gexp, glog, AF.Exp, bias=negmx, scale=1.0))
                    ve.wait_ge(ss.h, w_ge)
                    ve.tensor_reduce(gsum, gexp, axis=AX.X, op=ALU.add)
                    ve.reciprocal(gsum, gsum)
                    ve.tensor_scalar_mul(probas, gexp, gsum)
                    ve.tensor_tensor(biased, probas, bbb[:, 0:NE], ALU.add)
                    for e_ in range(NE):
                        ve.tensor_scalar(sq_scr[:, 0:NE], biased, biased[:, e_:e_ + 1],
                                         0.0, ALU.is_gt, ALU.add)
                        ve.tensor_reduce(cnt, sq_scr[:, 0:NE], axis=AX.X, op=ALU.add)
                        ve.tensor_scalar(wtm[:, e_:e_ + 1], cnt, float(TOPK) - 0.5, 1.0,
                                         ALU.is_lt, ALU.mult)
                    ve.tensor_tensor(wtm[:, 0:NE], wtm[:, 0:NE], probas, ALU.mult)
                    ve.tensor_reduce(wsum, wtm[:, 0:NE], axis=AX.X, op=ALU.add)
                    ve.reciprocal(wsum, wsum)
                    for e_ in range(NE):
                        ve.tensor_scalar_mul(wtm[:, e_:e_ + 1], wtm[:, e_:e_ + 1], wsum)
                    w_wt = sv.inc(ve.tensor_copy(wtmb[:, 0:NE], wtm[:, 0:NE]))
                    te.wait_ge(sv.h, w_wt)
                    if dbg and hf == 0:
                        sy.wait_ge(sv.h, w_wt)
                        so.inc(sy.dma_start(out=dbg_wtm[i], in_=wtm[:, 0:NE]), 16)
                        wdx = so.inc(sy.dma_start(out=dbg_gsc[i], in_=gsc[:, :]), 16)
                        sy.wait_ge(so.h, wdx)
                    w_tr = None
                    for e_ in range(NE):
                        w_tr = sp.inc(te.transpose(
                            PSbf[0:1, 2 * 1024 + e_ * 128: 2 * 1024 + (e_ + 1) * 128],
                            wtmb[:, e_:e_ + 1], ident_bf[:]))
                    ve.wait_ge(sp.h, w_tr)
                    w_wr = None
                    for e_ in range(NE):
                        w_wr = ve.tensor_copy(
                            wrow[0:1, e_ * NHALF + i * 128: e_ * NHALF + (i + 1) * 128],
                            PSbf[0:1, 2 * 1024 + e_ * 128: 2 * 1024 + (e_ + 1) * 128])
                    w_wr = sv.inc(w_wr)
                    te.wait_ge(sv.h, w_wr)
                w_zero = sv.inc(ve.memset(outacc[:], 0.0))
                te.wait_ge(sv.h, w_zero)

                def slice_src(dram, colt):
                    return dram.rearrange("(k p) m -> p k m", p=128)[:, :, colt * 128:(colt + 1) * 128]

                def issue_h(e2, ft):
                    pp = ft % 2
                    s = gsl[pp]
                    gp.wait_ge(s.h, s.n)
                    gp.wait_ge(sp.h, hguard[pp])
                    src1 = slice_src(ws1_in if e2 < 0 else w1_in[e2], ft)
                    srcg = slice_src(wsg_in if e2 < 0 else wg_in[e2], ft)
                    s.n = s.inc(gp.dma_start(
                        out=w1sl(pp).rearrange("p (k n) -> p k n", k=DT), in_=src1), 16)
                    gp.wait_ge(s.h, s.n)
                    s.n = s.inc(gp.dma_start(
                        out=wgsl(pp).rearrange("p (k n) -> p k n", k=DT), in_=srcg), 16)
                    return s.n

                def issue_o(e2, dt):
                    pp = dt % 2
                    s = gsl[pp]
                    gp.wait_ge(s.h, s.n)
                    gp.wait_ge(sp.h, oguard[pp])
                    src2 = slice_src(ws2_in if e2 < 0 else w2_in[e2], dt)
                    s.n = s.inc(gp.dma_start(
                        out=w2sl(pp).rearrange("p (k n) -> p k n", k=FT), in_=src2), 16)
                    return s.n

                hguard = [0, 0]
                oguard = [0, 0]
                hunit = [0]; ounit = [0]
                hq_done = {}; oq_done = {}
                pend = {}
                pend[("h", -1, 0)] = issue_h(-1, 0)
                for e_ in range(-1, NE):
                    if e_ >= 0:
                        se.wait_ge(sv.h, sv.n)
                        te.wait_ge(sv.h, sv.n)
                        for c in range(HT):
                            bnk = 6 + (c % 2)
                            w_b1 = sp.inc(te.matmul(psb(bnk, 512), ones_row[:],
                                                    wrow[0:1, e_ * NHALF + c * 512: e_ * NHALF + (c + 1) * 512],
                                                    start=True, stop=True))
                            se.wait_ge(sp.h, w_b1)
                            w_b2 = ss.inc(se.copy(wbc[:, c * 512:(c + 1) * 512], psb(bnk, 512)))
                            te.wait_ge(ss.h, w_b2)
                    for ft in range(FT):
                        pp = ft % 2
                        s = gsl[pp]
                        te.wait_ge(s.h, pend[("h", e_, ft)])
                        if ft + 1 < FT:
                            pend[("h", e_, ft + 1)] = issue_h(e_, ft + 1)
                        elif True:
                            pend[("o", e_, 0)] = issue_o(e_, 0)
                        for c in range(HT):
                            u = hunit[0]; hunit[0] += 1
                            b0, b1 = (0, 1) if u % 2 == 0 else (2, 3)
                            sl = u % 2
                            te.wait_ge(sv.h, hq_done.get(u - 2, 0))
                            lastH = None
                            for k in range(DT):
                                te.matmul(psb(b0, 512), w1sl(pp)[:, k * 128:(k + 1) * 128],
                                          zdm[:, k * NHALF + c * 512: k * NHALF + (c + 1) * 512],
                                          start=(k == 0), stop=(k == DT - 1))
                                lastH = te.matmul(psb(b1, 512), wgsl(pp)[:, k * 128:(k + 1) * 128],
                                                  zdm[:, k * NHALF + c * 512: k * NHALF + (c + 1) * 512],
                                                  start=(k == 0), stop=(k == DT - 1))
                            w_h = sp.inc(lastH)
                            se.wait_ge(sp.h, w_h)
                            w_si = ss.inc(se.activation(silub[:, sl * 512:sl * 512 + 512],
                                                        psb(b0, 512), AF.Silu,
                                                        bias=0.0, scale=1.0))
                            ve.wait_ge(ss.h, w_si)
                            w_m1 = sv.inc(ve.tensor_tensor(tmulb[:, sl * 512:sl * 512 + 512],
                                                           silub[:, sl * 512:sl * 512 + 512],
                                                           psb(b1, 512), ALU.mult))
                            if e_ >= 0:
                                w_m2 = sv.inc(ve.tensor_tensor(
                                    hbuf[:, ft * NHALF + c * 512: ft * NHALF + (c + 1) * 512],
                                    tmulb[:, sl * 512:sl * 512 + 512],
                                    wbc[:, c * 512:(c + 1) * 512], ALU.mult))
                            else:
                                w_m2 = sv.inc(ve.tensor_copy(
                                    hbuf[:, ft * NHALF + c * 512: ft * NHALF + (c + 1) * 512],
                                    tmulb[:, sl * 512:sl * 512 + 512]))
                            hq_done[u] = w_m2
                        hguard[pp] = sp.n
                    for dt in range(DT):
                        pp = dt % 2
                        s = gsl[pp]
                        if dt == 0:
                            te.wait_ge(sv.h, sv.n)
                        te.wait_ge(s.h, pend[("o", e_, dt)])
                        if dt + 1 < DT:
                            pend[("o", e_, dt + 1)] = issue_o(e_, dt + 1)
                        elif e_ + 1 < NE:
                            pend[("h", e_ + 1, 0)] = issue_h(e_ + 1, 0)
                        elif hf == 0:
                            pass
                        for c in range(HT):
                            u = ounit[0]; ounit[0] += 1
                            bo = 4 + (u % 4)
                            te.wait_ge(sv.h, oq_done.get(u - 3, 0))
                            lastO = None
                            for k in range(FT):
                                lastO = te.matmul(psb(bo, 512),
                                                  w2sl(pp)[:, k * 128:(k + 1) * 128],
                                                  hbuf[:, k * NHALF + c * 512: k * NHALF + (c + 1) * 512],
                                                  start=(k == 0), stop=(k == FT - 1))
                            w_o2 = sp.inc(lastO)
                            ve.wait_ge(sp.h, w_o2)
                            w_a = sv.inc(ve.tensor_tensor(
                                outacc[:, dt * NHALF + c * 512: dt * NHALF + (c + 1) * 512],
                                outacc[:, dt * NHALF + c * 512: dt * NHALF + (c + 1) * 512],
                                psb(bo, 512), ALU.add))
                            oq_done[u] = w_a
                        oguard[pp] = sp.n

                if dbg and hf == 0:
                    tile_barrier()
                    sy.wait_ge(sv.h, sv.n)
                    sy.wait_ge(ss.h, ss.n)
                    so.inc(sy.dma_start(out=dbg_gw[:, :], in_=gwbuf[:, :]), 16)
                    so.inc(sy.dma_start(out=dbg_hbuf[:, :], in_=hbuf[:, :]), 16)
                    so.inc(sy.dma_start(out=dbg_outacc[:, :], in_=outacc[:, :]), 16)
                    so.inc(sy.dma_start(out=dbg_wrow[:, :], in_=wrow[:, :]), 16)
                    wdbg = so.inc(sy.dma_start(out=dbg_wbc[:, :], in_=wbc[:, :]), 16)
                    sy.wait_ge(so.h, wdbg)
                w_cast = sv.inc(ve.tensor_copy(hbuf[:, 0:DT * NHALF], outacc[:, 0:DT * NHALF]))
                te.wait_ge(sv.h, w_cast)
                for i in range(NHALF // 128):
                    tile_barrier()
                    jt = hf * (NHALF // 128) + i
                    te.wait_ge(sv.h, sv.n)
                    done = 0
                    while done < DT:
                        grp = min(8, DT - done)
                        bank = (done // 8) % 2
                        lastT3 = None
                        for g2 in range(grp):
                            dtl = done + g2
                            lastT3 = te.transpose(
                                PSbf[:, bank * 1024 + g2 * 128: bank * 1024 + (g2 + 1) * 128],
                                hbuf[:, dtl * NHALF + i * 128: dtl * NHALF + (i + 1) * 128],
                                ident_bf[:])
                        w_t4 = sp.inc(lastT3)
                        ve.wait_ge(sp.h, w_t4)
                        w_v4 = sv.inc(ve.tensor_copy(
                            ffntm[:, done * 128:(done + grp) * 128],
                            PSbf[:, bank * 1024: bank * 1024 + grp * 128]))
                        te.wait_ge(sv.h, w_v4)
                        done += grp
                    wx1b = ld.inc(sy.dma_start(out=x1b[:],
                                               in_=x1_sp[jt * 128:(jt + 1) * 128, :]), 16)
                    se.wait_ge(sv.h, sv.n)
                    ln_tile(ffntm[:], sq_scr[:, 0:D])
                    ve.wait_ge(ld.h, wx1b)
                    w_out = sv.inc(ve.tensor_tensor(x1b[:], x1b[:], sq_scr[:, 0:D],
                                                    ALU.add))
                    sy.wait_ge(sv.h, w_out)
                    wsto = so.inc(sy.dma_start(out=out_sh[jt * 128:(jt + 1) * 128, :],
                                               in_=x1b[:]), 16)
                    sy.wait_ge(so.h, wsto)

        sy.wait_ge(so.h, so.n)
        gp.wait_ge(gl.h, gl.n)
      except _StopBuild:
        sy.wait_ge(so.h, so.n)
        gp.wait_ge(gl.h, gl.n)
    return nc


def _rope_tables(N):
    inv = 1.0 / ROPE_BASE ** (np.arange(0, HD, 2, dtype=np.float32) / HD)
    th = np.arange(N, dtype=np.float32)[:, None] * inv[None, :]
    return np.cos(th).astype(np.float32), np.sin(th).astype(np.float32)


def _host_prep(inputs, cfg, core):
    N, D, H, NE = cfg["N"], cfg["D"], cfg["H"], cfg["NE"]
    b, sub = core // 2, core % 2
    NQT = (N // 2) // 128
    qtiles = [2 * j + sub for j in range(NQT)]
    qidx = np.concatenate([np.arange(t * 128, (t + 1) * 128) for t in qtiles])

    x = np.asarray(inputs["L0_attn_x"][b], dtype=np.float32)
    d = {}
    d["x_full"] = np.ascontiguousarray(x)
    d["x_q"] = np.ascontiguousarray(x[qidx])
    for nm, key in [("wq_in", "L0_attn_Wq"), ("wk_in", "L0_attn_Wk"),
                    ("wv_in", "L0_attn_Wv"), ("wo_in", "L0_attn_Wo"),
                    ("wb_in", "L0_attn_Wb"), ("w1_in", "L0_ffn_W1"),
                    ("wg_in", "L0_ffn_Wg"), ("w2_in", "L0_ffn_W2"),
                    ("ws1_in", "L0_ffn_Ws1"), ("wsg_in", "L0_ffn_Wsg"),
                    ("ws2_in", "L0_ffn_Ws2"), ("gatew_in", "L0_ffn_gate_w")]:
        d[nm] = np.ascontiguousarray(np.asarray(inputs[key], dtype=np.float32))
    d["gateb_bc"] = np.ascontiguousarray(np.broadcast_to(
        np.asarray(inputs["L0_ffn_gate_b"], np.float32)[None, :], (128, NE)))
    d["biases_bc"] = np.ascontiguousarray(np.broadcast_to(
        np.asarray(inputs["L0_ffn_biases"], np.float32)[None, :], (128, NE)))
    coskv, sinkv = _rope_tables(N)
    coskv = np.ascontiguousarray(np.tile(coskv, (1, H)))
    sinkv = np.ascontiguousarray(np.tile(sinkv, (1, H)))
    d["cos_kv"] = coskv
    d["sin_kv"] = sinkv
    d["cos_q"] = np.ascontiguousarray(coskv[qidx])
    d["sin_q"] = np.ascontiguousarray(sinkv[qidx])
    d["ident_in"] = np.eye(128, dtype=np.float32)
    tril = np.tril(np.ones((128, 128), np.float32), -1)
    tm = np.zeros((7, 128, H * 128), np.float32)
    for k in range(7):
        bs = 2 ** (k + 1)
        blk = np.zeros((128, 128), np.float32)
        for bb in range(128 // bs):
            blk[bb * bs:(bb + 1) * bs, bb * bs:(bb + 1) * bs] = 1
        tm[k] = np.tile(tril * blk, (1, H))
    d["tmasks_in"] = tm
    d["irep_in"] = np.tile(np.eye(128, dtype=np.float32), (1, H))
    d["tworep_in"] = 2.0 * d["irep_in"]
    pal = np.zeros((4, 128, 256), np.float32)
    tri_kq = np.tril(np.ones((128, 128), np.float32)).T
    for c in range(4):
        for blk_ in range(2):
            rel = 2 * blk_ + sub - c
            if rel > 0:
                pal[c, :, blk_ * 128:(blk_ + 1) * 128] = 1.0
            elif rel == 0:
                pal[c, :, blk_ * 128:(blk_ + 1) * 128] = tri_kq
    d["pal_in"] = pal
    d["_qidx"] = qidx
    return d


def kernel(**inputs):
    cfg = FULL
    nc = build_program(cfg)
    in_maps, qidxs = [], []
    for core in range(8):
        d = _host_prep(inputs, cfg, core)
        qidxs.append(d.pop("_qidx"))
        in_maps.append(d)
    res = run_bass_kernel_spmd(nc, in_maps, core_ids=list(range(8)))
    x = np.asarray(inputs["L0_attn_x"], dtype=np.float32)
    out = np.zeros_like(x)
    for core in range(8):
        out[core // 2][qidxs[core]] = res.results[core]["out_sh"]
    return out


# version 10
# speedup vs baseline: 1.2069x; 1.2069x over previous
import numpy as np
from contextlib import ExitStack

import concourse.bass as bass
import concourse.mybir as mybir
from concourse.bass_utils import run_bass_kernel_spmd

F32 = mybir.dt.float32
BF16 = mybir.dt.bfloat16
AF = mybir.ActivationFunctionType
ALU = mybir.AluOpType
AX = mybir.AxisListType

FULL = dict(N=4096, D=2048, H=16, NE=7, TOPK=3, DFF=2048)
SMALL = dict(N=2048, D=512, H=4, NE=3, TOPK=2, DFF=512)

HD = 128
EPS_LN = 1e-5
EPS_L2 = 1e-6
ROPE_BASE = 10000.0


class Sem:
    def __init__(self, nc, name):
        self.h = nc.semaphore(name).__enter__()
        self.n = 0

    def inc(self, instr, k=1):
        if isinstance(instr, int):
            return instr
        instr.then_inc(self.h, k)
        self.n += k
        return self.n


class SyncEng:

    def __init__(self, eng, sem):
        object.__setattr__(self, "_e", eng)
        object.__setattr__(self, "_s", sem)

    def wait_ge(self, h, v):
        return self._e.wait_ge(h, v)

    def __getattr__(self, name):
        f = getattr(self._e, name)

        def wrap(*a, **kw):
            self._e.wait_ge(self._s.h, self._s.n)
            instr = f(*a, **kw)
            return self._s.inc(instr)

        return wrap


class _StopBuild(Exception):
    pass


def build_program(cfg, stop_phase=None, dbg=False):
    N, D, H, NE, TOPK, DFF = (cfg["N"], cfg["D"], cfg["H"], cfg["NE"],
                              cfg["TOPK"], cfg["DFF"])
    NQ = N // 2
    NT = N // 128
    NQT = NQ // 128
    DT = D // 128
    FT = DFF // 128
    QC = NQ // 256
    HB = H * 128
    SCALE = HD ** -0.5
    assert H * HD == D and DT == H

    nc = bass.Bass("TRN2", target_bir_lowering=False)

    dI = lambda nm, sh: nc.dram_tensor(nm, sh, F32, kind="ExternalInput").ap()
    dO = lambda nm, sh: nc.dram_tensor(nm, sh, F32, kind="ExternalOutput").ap()
    if dbg:
        dS = lambda nm, sh, dt=BF16: nc.dram_tensor(nm, sh, dt, kind="ExternalOutput").ap()
    else:
        dS = lambda nm, sh, dt=BF16: nc.dram_tensor(nm, sh, dt).ap()

    x_full = dI("x_full", [N, D])
    x_q = dI("x_q", [NQ, D])
    wq_in = dI("wq_in", [D, D])
    wk_in = dI("wk_in", [D, D])
    wv_in = dI("wv_in", [D, D])
    wo_in = dI("wo_in", [D, D])
    wb_in = dI("wb_in", [D, H])
    w1_in = dI("w1_in", [NE, D, DFF])
    wg_in = dI("wg_in", [NE, D, DFF])
    w2_in = dI("w2_in", [NE, DFF, D])
    ws1_in = dI("ws1_in", [D, DFF])
    wsg_in = dI("wsg_in", [D, DFF])
    ws2_in = dI("ws2_in", [DFF, D])
    gatew_in = dI("gatew_in", [D, NE])
    gateb_bc = dI("gateb_bc", [128, NE])
    biases_bc = dI("biases_bc", [128, NE])
    cos_kv = dI("cos_kv", [N, H * 64])
    sin_kv = dI("sin_kv", [N, H * 64])
    cos_q = dI("cos_q", [NQ, H * 64])
    sin_q = dI("sin_q", [NQ, H * 64])
    ident_in = dI("ident_in", [128, 128])
    tmasks_in = dI("tmasks_in", [7, 128, HB])
    irep_in = dI("irep_in", [128, HB])
    tworep_in = dI("tworep_in", [128, HB])
    pal_in = dI("pal_in", [4, 128, 256])

    out_sh = dO("out_sh", [NQ, D])

    q_sp = dS("q_sp", [NQT, DT, 128, 128])
    ydm_sp = dS("ydm_sp", [NT, DT, 128, 128])
    kdm_sp = dS("kdm_sp", [NT, H, 128, 128])
    ktdm_sp = dS("ktdm_sp", [NT, H, 128, 128])
    kttm_sp = dS("kttm_sp", [NT, 128, D])
    vtm_sp = dS("vtm_sp", [NT, 128, D])
    vn_sp = dS("vn_sp", [NT, H, 128, 128])
    x1_sp = dS("x1_sp", [NQ, D], F32)
    if dbg:
        dbg_hbuf = dS("dbg_hbuf", [128, (DFF // 128) * (NQ // 2)], BF16)
        dbg_outacc = dS("dbg_outacc", [128, DT * (NQ // 2)], F32)
        dbg_wrow = dS("dbg_wrow", [1, NE * (NQ // 2)], BF16)
        dbg_wbc = dS("dbg_wbc", [128, NQ // 2], BF16)
        dbg_gw = dS("dbg_gw", [128, DT * NE], BF16)
        dbg_gw0 = dS("dbg_gw0", [128, DT * NE], BF16)
        dbg_wtm = dS("dbg_wtm", [(NQ // 2) // 128, 128, NE], F32)
        dbg_gsc = dS("dbg_gsc", [(NQ // 2) // 128, 128, 4 * NE + 4], F32)

    with ExitStack() as ctx:
      try:
        E = ctx.enter_context
        te, sy, gp = nc.tensor, nc.sync, nc.gpsimd

        ld = Sem(nc, "ld")
        gl = Sem(nc, "gl")
        sv = Sem(nc, "sv")
        ss = Sem(nc, "ss")
        sp = Sem(nc, "sp")
        so = Sem(nc, "so")
        gla = Sem(nc, "gla")
        glb = Sem(nc, "glb")
        gsl = [gla, glb]
        ve = SyncEng(nc.vector, sv)
        se = SyncEng(nc.scalar, ss)

        sq_scr = E(nc.sbuf_tensor("sq_scr", [128, D], F32))
        rp_scr = E(nc.sbuf_tensor("rp_scr", [128, 128], F32))
        stat = E(nc.sbuf_tensor("stat", [128, 8], F32))
        ident_bf = E(nc.sbuf_tensor("ident_bf", [128, 128], BF16))
        ones_col = E(nc.sbuf_tensor("ones_col", [128, 1], BF16))
        ones_row = E(nc.sbuf_tensor("ones_row", [1, 128], BF16))
        biasM = E(nc.sbuf_tensor("biasM", [128, H], F32))
        mxabs = E(nc.sbuf_tensor("mxabs", [128, H], F32))
        small_bf = E(nc.sbuf_tensor("small_bf", [128, 128], BF16))
        eps_ln = E(nc.sbuf_tensor("eps_ln", [128, 1], F32))
        gwbuf = E(nc.sbuf_tensor("gwbuf", [128, DT * NE], BF16))
        eps_l2 = E(nc.sbuf_tensor("eps_l2", [128, 1], F32))
        PS = E(nc.psum_tensor("PS", [128, 4096], F32))
        psb = lambda i, w=512: PS[:, i * 512:i * 512 + w]
        PSbf = PS.ap().bitcast(BF16)

        def gdma(**kw):
            gp.wait_ge(gl.h, gl.n)
            gp.wait_ge(sp.h, sp.n)
            return gl.inc(gp.dma_start(**kw), 16)

        gdma(out=ident_bf[:], in_=ident_in)
        gdma(out=gwbuf.ap().rearrange("p (k n) -> p k n", k=DT),
             in_=gatew_in.rearrange("(k p) n -> p k n", p=128))
        const_gl = gl.n
        ve.memset(ones_col[:], 1.0)
        ve.memset(eps_ln[:], EPS_LN)
        ve.memset(eps_l2[:], EPS_L2)
        ve.memset(ones_row[:], 1.0)
        w_mx0 = sv.inc(ve.memset(mxabs[:, 0:H], 0.0))
        te.wait_ge(gl.h, const_gl)
        te.wait_ge(sv.h, w_mx0)
        se.wait_ge(sv.h, w_mx0)

        def tile_barrier():
            c_sv, c_ss, c_sp, c_so, c_ld, c_gl = sv.n, ss.n, sp.n, so.n, ld.n, gl.n
            nc.vector.wait_ge(ss.h, c_ss)
            nc.vector.wait_ge(sp.h, c_sp)
            nc.vector.wait_ge(so.h, c_so)
            nc.scalar.wait_ge(sv.h, c_sv)
            nc.scalar.wait_ge(sp.h, c_sp)
            nc.scalar.wait_ge(so.h, c_so)
            te.wait_ge(sv.h, c_sv)
            te.wait_ge(ss.h, c_ss)
            sy.wait_ge(sv.h, c_sv)
            sy.wait_ge(ss.h, c_ss)
            sy.wait_ge(sp.h, c_sp)
            gp.wait_ge(sp.h, c_sp)

        def ln_tile(xin, yout):
            s1 = stat[:, 0:1]; s2 = stat[:, 1:2]; m = stat[:, 2:3]
            var = stat[:, 3:4]; rstd = stat[:, 4:5]
            ve.tensor_reduce(s1, xin, axis=AX.X, op=ALU.add)
            w_s = ss.inc(se.activation(sq_scr[:, 0:D], xin, AF.Square, bias=0.0,
                                       scale=1.0, accum_out=s2))
            ve.wait_ge(ss.h, w_s)
            ve.tensor_scalar_mul(m, s1, 1.0 / D)
            ve.tensor_scalar_mul(var, s2, 1.0 / D)
            ve.tensor_tensor(s1, m, m, ALU.mult)
            w_v = sv.inc(ve.tensor_tensor(var, var, s1, ALU.subtract))
            se.wait_ge(sv.h, w_v)
            w_s2 = ss.inc(se.activation(stat[:, 6:7], var, AF.Sqrt, bias=eps_ln[:, 0:1], scale=1.0))
            ve.wait_ge(ss.h, w_s2)
            ve.reciprocal(rstd, stat[:, 6:7])
            return sv.inc(ve.tensor_scalar(yout, xin, m, rstd, ALU.subtract, ALU.mult))

        def transposes(src_bf, col_tiles, dst, dst_col0, pbank0):
            nt_ = len(col_tiles)
            te.wait_ge(sv.h, sv.n)
            te.wait_ge(ss.h, ss.n)
            nc.vector.wait_ge(so.h, so.n)
            done = 0
            while done < nt_:
                grp = min(8, nt_ - done)
                bank = pbank0 + (done // 8) % 2
                for i in range(grp):
                    c = col_tiles[done + i]
                    last = te.transpose(
                        PSbf[:, bank * 1024 + i * 128: bank * 1024 + (i + 1) * 128],
                        src_bf[:, c * 128:(c + 1) * 128], ident_bf[:])
                w_t = sp.inc(last)
                ve.wait_ge(sp.h, w_t)
                w_v = sv.inc(ve.tensor_copy(
                    dst[:, (dst_col0 + done) * 128: (dst_col0 + done + grp) * 128],
                    PSbf[:, bank * 1024: bank * 1024 + grp * 128]))
                te.wait_ge(sv.h, w_v)
                done += grp
            return sv.n

        def rope_tile(src, dst, cosap, sinap, t1buf, t2buf):
            HQ = H * 64
            v3 = lambda ap: ap.rearrange("p (h q d) -> p h q d", h=H, q=2)
            X1 = v3(src)[:, :, 0, :]
            X2 = v3(src)[:, :, 1, :]
            O1 = v3(dst)[:, :, 0, :]
            O2 = v3(dst)[:, :, 1, :]
            c3 = lambda flat: flat.rearrange("p (h d) -> p h d", h=H)
            COS = c3(cosap); SIN = c3(sinap)
            T1 = c3(t1buf.ap()[:, 0:HQ])
            T2 = c3(t2buf.ap()[:, 0:HQ])
            ve.tensor_tensor(T1, X1, COS, ALU.mult)
            ve.tensor_tensor(T2, X2, SIN, ALU.mult)
            ve.tensor_tensor(O1, T1, T2, ALU.subtract)
            ve.tensor_tensor(T1, X1, SIN, ALU.mult)
            ve.tensor_tensor(T2, X2, COS, ALU.mult)
            return sv.inc(ve.tensor_tensor(O2, T1, T2, ALU.add))

        with ExitStack() as px:
            Ex = px.enter_context
            wbuf = Ex(nc.sbuf_tensor("wbuf", [128, DT * D], BF16))
            wbbuf = Ex(nc.sbuf_tensor("wbbuf", [128, DT * H], BF16))
            xbuf_ = Ex(nc.sbuf_tensor("xbuf", [128, 2 * D], F32))
            xbuf = lambda i: xbuf_[:, (i % 2) * D:(i % 2) * D + D]
            ybuf = Ex(nc.sbuf_tensor("ybuf", [128, D], BF16))
            ydm = Ex(nc.sbuf_tensor("ydm", [128, D], BF16))
            prj = Ex(nc.sbuf_tensor("prj", [128, D], F32))
            rop = Ex(nc.sbuf_tensor("rop", [128, D], F32))
            ropb = Ex(nc.sbuf_tensor("ropb", [128, D], BF16))
            dmb = Ex(nc.sbuf_tensor("dmb", [128, D], BF16))
            ktb = Ex(nc.sbuf_tensor("ktb", [128, D], BF16))
            HQ = H * 64
            ctab_ = Ex(nc.sbuf_tensor("ctab", [128, 4 * HQ], F32))
            ctab2 = lambda p, t: ctab_[:, (p * 2 + t) * HQ:(p * 2 + t + 1) * HQ]
            rp1_ = Ex(nc.sbuf_tensor("rp1", [128, HQ], F32))
            rp2_ = Ex(nc.sbuf_tensor("rp2", [128, HQ], F32))
            l2s = Ex(nc.sbuf_tensor("l2s", [128, D], F32))
            beta2 = Ex(nc.sbuf_tensor("beta2", [128, H], F32))
            lstat = Ex(nc.sbuf_tensor("lstat", [128, H], F32))

            def load_w(dramw, buf):
                return gdma(
                    out=buf.ap().rearrange("p (k n) -> p k n", k=DT),
                    in_=dramw.rearrange("(k p) n -> p k n", p=128))

            def proj_tile(wsrc, out_f32, width, pb0):
                nchunk = width // 512 if width >= 512 else 1
                cw0 = min(512, width)
                last = None
                for c in range(nchunk):
                    for k in range(DT):
                        last = te.matmul(
                            psb(pb0 + c, cw0),
                            ydm[:, k * 128:(k + 1) * 128],
                            wsrc[:, k * width + c * 512: k * width + c * 512 + cw0],
                            start=(k == 0), stop=(k == DT - 1))
                w_t = sp.inc(last)
                ve.wait_ge(sp.h, w_t)
                lastv = None
                for c in range(nchunk):
                    lastv = ve.tensor_copy(out_f32[:, c * 512:c * 512 + cw0], psb(pb0 + c, cw0))
                w_v = sv.inc(lastv)
                te.wait_ge(sv.h, w_v)
                return w_v

            xfree_v = [0, 0]; xfree_s = [0, 0]
            wx = [0, 0]; wct = [0, 0]
            ctab_free = [0, 0]
            st8 = dict(tpin=0, tpout=0, tpout2=0, sg=0, ktb=0, sq=0, kspill=0)

            def qk_loads(jt, p, srcdram, costab, sintab):
                sy.wait_ge(sv.h, xfree_v[p])
                sy.wait_ge(ss.h, xfree_s[p])
                wx[p] = ld.inc(sy.dma_start(out=xbuf(p)[:],
                                            in_=srcdram[jt * 128:(jt + 1) * 128, :]), 16)
                sy.wait_ge(sv.h, ctab_free[p])
                ld.inc(sy.dma_start(out=ctab2(p, 0),
                                    in_=costab[jt * 128:(jt + 1) * 128, :]), 16)
                wct[p] = ld.inc(sy.dma_start(out=ctab2(p, 1),
                                             in_=sintab[jt * 128:(jt + 1) * 128, :]), 16)

            wq_done = load_w(wq_in, wbuf)
            qk_loads(0, 0, x_q, cos_q, sin_q)
            for jt in range(NQT):
                p = jt % 2
                if jt + 1 < NQT:
                    qk_loads(jt + 1, 1 - p, x_q, cos_q, sin_q)
                nc.vector.wait_ge(ld.h, wx[p])
                nc.scalar.wait_ge(ld.h, wx[p])
                nc.vector.wait_ge(sp.h, st8["tpin"])
                ln_tile(xbuf(p)[:], ybuf[:])
                xfree_v[p] = sv.n
                xfree_s[p] = ss.n
                transposes(ybuf, list(range(DT)), ydm, 0, 0)
                st8["tpin"] = sp.n
                if jt == 0:
                    te.wait_ge(gl.h, wq_done)
                proj_tile(wbuf, prj, D, 2)
                nc.vector.wait_ge(ld.h, wct[p])
                w_rope = rope_tile(prj, rop, ctab2(p, 0), ctab2(p, 1), rp1_, rp2_)
                ctab_free[p] = w_rope
                r3 = rop.ap().rearrange("p (h d) -> p h d", h=H)
                ve.tensor_reduce(lstat[:, 0:H], r3, axis=AX.X, op=ALU.max,
                                 apply_absolute_value=True)
                ve.tensor_tensor(mxabs[:, 0:H], mxabs[:, 0:H], lstat[:, 0:H], ALU.max)
                nc.vector.wait_ge(sp.h, st8["tpout"])
                w_cast = sv.inc(ve.tensor_copy(ropb[:], rop[:]))
                w_tp = transposes(ropb, list(range(DT)), dmb, 0, 6)
                st8["tpout"] = sp.n
                sy.wait_ge(sv.h, w_tp)
                so.inc(sy.dma_start(
                    out=q_sp[jt].rearrange("t p n -> p t n"),
                    in_=dmb.ap().rearrange("p (t n) -> p t n", t=DT)), 16)

            w_mxc = sv.inc(ve.tensor_copy(small_bf[:, 0:H], mxabs[:, 0:H]))
            te.wait_ge(sv.h, w_mxc)
            w_t1 = sp.inc(te.transpose(PSbf[0:H, 0:128], small_bf[:, 0:H], ident_bf[:]))
            ve.wait_ge(sp.h, w_t1)
            ve.tensor_reduce(stat[0:H, 0:1], PSbf[0:H, 0:128], axis=AX.X, op=ALU.max)
            ve.tensor_scalar_mul(stat[0:H, 1:2], stat[0:H, 0:1], -2.0)
            w_mxs = sv.inc(ve.tensor_copy(small_bf[0:H, 0:1], stat[0:H, 1:2]))
            te.wait_ge(sv.h, w_mxs)
            w_t2 = sp.inc(te.transpose(PSbf[0:1, 0:H], small_bf[0:H, 0:1], ident_bf[0:H, 0:H]))
            ve.wait_ge(sp.h, w_t2)
            w_mr = sv.inc(ve.tensor_copy(small_bf[0:1, 0:H], PSbf[0:1, 0:H]))
            te.wait_ge(sv.h, w_mr)
            w_bc = sp.inc(te.matmul(psb(1, H), ones_row[:], small_bf[0:1, 0:H],
                                    start=True, stop=True))
            se.wait_ge(sp.h, w_bc)
            w_bM = ss.inc(se.copy(biasM[:, 0:H], psb(1, H)))

            wk_done = load_w(wk_in, wbuf)
            wb_done = gdma(out=wbbuf.ap().rearrange("p (k n) -> p k n", k=DT),
                           in_=wb_in.rearrange("(k p) n -> p k n", p=128))
            qk_loads(0, 0, x_full, cos_kv, sin_kv)
            for kt in range(NT):
                p = kt % 2
                if kt + 1 < NT:
                    qk_loads(kt + 1, 1 - p, x_full, cos_kv, sin_kv)
                nc.vector.wait_ge(ld.h, wx[p])
                nc.scalar.wait_ge(ld.h, wx[p])
                nc.vector.wait_ge(sp.h, st8["tpin"])
                ln_tile(xbuf(p)[:], ybuf[:])
                xfree_v[p] = sv.n
                xfree_s[p] = ss.n
                w_tpin = transposes(ybuf, list(range(DT)), ydm, 0, 0)
                st8["tpin"] = sp.n
                sy.wait_ge(sv.h, w_tpin)
                so.inc(sy.dma_start(
                    out=ydm_sp[kt].rearrange("t p n -> p t n"),
                    in_=ydm.ap().rearrange("p (t n) -> p t n", t=DT)), 16)
                if kt == 0:
                    te.wait_ge(gl.h, wb_done)
                nc.vector.wait_ge(ss.h, st8["sq"])
                proj_tile(wbuf, prj, D, 2)
                se.wait_ge(sv.h, sv.n)
                w_sq = ss.inc(se.activation(l2s[:, 0:D], prj[:], AF.Square,
                                            bias=0.0, scale=1.0))
                st8["sq"] = w_sq
                ve.wait_ge(ss.h, w_sq)
                r3 = l2s.ap().rearrange("p (h d) -> p h d", h=H)
                w_red = sv.inc(ve.tensor_reduce(lstat[:, 0:H], r3, axis=AX.X, op=ALU.add))
                se.wait_ge(sv.h, w_red)
                w_sqr = ss.inc(se.activation(lstat[:, 0:H], lstat[:, 0:H], AF.Sqrt,
                                             bias=eps_l2[:, 0:1], scale=1.0))
                ve.wait_ge(ss.h, w_sqr)
                ve.reciprocal(lstat[:, 0:H], lstat[:, 0:H])
                for h in range(H):
                    ve.tensor_scalar_mul(prj[:, h * 128:(h + 1) * 128],
                                         prj[:, h * 128:(h + 1) * 128], lstat[:, h:h + 1])
                nc.vector.wait_ge(ld.h, wct[p])
                w_rope = rope_tile(prj, rop, ctab2(p, 0), ctab2(p, 1), rp1_, rp2_)
                ctab_free[p] = w_rope
                te.wait_ge(ss.h, st8["sg"])
                lastb = None
                for k in range(DT):
                    lastb = te.matmul(psb(6, H), ydm[:, k * 128:(k + 1) * 128],
                                      wbbuf[:, k * H:(k + 1) * H],
                                      start=(k == 0), stop=(k == DT - 1))
                w_bt = sp.inc(lastb)
                se.wait_ge(sp.h, w_bt)
                se.wait_ge(sv.h, st8["ktb"])
                w_sg = ss.inc(se.activation(beta2[:, 0:H], psb(6, H), AF.Sigmoid,
                                            bias=0.0, scale=1.0))
                st8["sg"] = w_sg
                ve.wait_ge(ss.h, w_sg)
                ve.tensor_scalar_mul(beta2[:, 0:H], beta2[:, 0:H], 2.0)
                nc.vector.wait_ge(sp.h, st8["tpout2"])
                nc.vector.wait_ge(so.h, st8["kspill"])
                lastk = None
                for h in range(H):
                    lastk = ve.tensor_scalar_mul(ktb[:, h * 128:(h + 1) * 128],
                                                 rop[:, h * 128:(h + 1) * 128],
                                                 beta2[:, h:h + 1])
                st8["ktb"] = lastk
                nc.vector.wait_ge(sp.h, st8["tpout"])
                w_cst = sv.inc(ve.tensor_copy(ropb[:], rop[:]))
                w_tp = transposes(ropb, list(range(H)), dmb, 0, 6)
                st8["tpout"] = sp.n
                sy.wait_ge(sv.h, w_tp)
                so.inc(sy.dma_start(
                    out=kdm_sp[kt].rearrange("t p n -> p t n"),
                    in_=dmb.ap().rearrange("p (t n) -> p t n", t=H)), 16)
                w_tp2 = transposes(ktb, list(range(H)), dmb, 0, 6)
                st8["tpout2"] = sp.n
                sy.wait_ge(sv.h, w_tp2)
                so.inc(sy.dma_start(
                    out=ktdm_sp[kt].rearrange("t p n -> p t n"),
                    in_=dmb.ap().rearrange("p (t n) -> p t n", t=H)), 16)
                so.inc(sy.dma_start(out=kttm_sp[kt], in_=ktb[:]), 16)
                st8["kspill"] = so.n
            sy.wait_ge(so.h, so.n)

            wv_done = load_w(wv_in, wbuf)
            tile_barrier()
            ydms = [ydm, dmb]
            vouts = [ropb, ktb]
            v_proj_done = [0, 0]
            v_cast_done = [0, 0]
            v_spill_done = [so.n, so.n]
            v_wl = [0, 0]

            def v_load(kt):
                p = kt % 2
                sy.wait_ge(sp.h, v_proj_done[p])
                v_wl[p] = ld.inc(sy.dma_start(
                    out=ydms[p].ap().rearrange("p (t n) -> p t n", t=DT),
                    in_=ydm_sp[kt].rearrange("t p n -> p t n")), 16)

            v_load(0)
            for kt in range(NT):
                p = kt % 2
                if kt + 1 < NT:
                    v_load(kt + 1)
                te.wait_ge(ld.h, v_wl[p])
                if kt == 0:
                    te.wait_ge(gl.h, wv_done)
                te.wait_ge(sv.h, v_cast_done[p])
                last = None
                for c in range(4):
                    for k in range(DT):
                        last = te.matmul(psb(4 * p + c, 512),
                                         ydms[p][:, k * 128:(k + 1) * 128],
                                         wbuf[:, k * D + c * 512: k * D + c * 512 + 512],
                                         start=(k == 0), stop=(k == DT - 1))
                w_t = sp.inc(last)
                v_proj_done[p] = w_t
                ve.wait_ge(sp.h, w_t)
                ve.wait_ge(so.h, v_spill_done[p])
                lastv = None
                for c in range(4):
                    lastv = ve.tensor_copy(vouts[p][:, c * 512:(c + 1) * 512],
                                           psb(4 * p + c, 512))
                w_cst = sv.inc(lastv)
                v_cast_done[p] = w_cst
                sy.wait_ge(sv.h, w_cst)
                v_spill_done[p] = so.inc(sy.dma_start(out=vtm_sp[kt], in_=vouts[p][:]), 16)
            sy.wait_ge(so.h, so.n)

        if stop_phase == "qkv":
            raise_stop = True
        if stop_phase == "qkv":
            raise _StopBuild
        with ExitStack() as px:
            Ex = px.enter_context
            tmasks_bf = Ex(nc.sbuf_tensor("tmasks_bf", [128, 7 * HB], BF16))
            irep_bf = Ex(nc.sbuf_tensor("irep_bf", [128, HB], BF16))
            tworep_bf = Ex(nc.sbuf_tensor("tworep_bf", [128, HB], BF16))
            kdm_c_ = Ex(nc.sbuf_tensor("kdm_c", [128, 2 * HB], BF16))
            kdm_c = lambda i: kdm_c_[:, i * HB:(i + 1) * HB]
            ktdm_c_ = Ex(nc.sbuf_tensor("ktdm_c", [128, 2 * HB], BF16))
            ktdm_c = lambda i: ktdm_c_[:, i * HB:(i + 1) * HB]
            kttm_c_ = Ex(nc.sbuf_tensor("kttm_c", [128, 2 * D], BF16))
            kttm_c = lambda i: kttm_c_[:, i * D:(i + 1) * D]
            vtm_c_ = Ex(nc.sbuf_tensor("vtm_c", [128, 2 * D], BF16))
            vtm_c = lambda i: vtm_c_[:, i * D:(i + 1) * D]
            Gsb = Ex(nc.sbuf_tensor("Gsb", [128, HB], BF16))
            Gm = Ex(nc.sbuf_tensor("Gm", [128, 7 * HB], BF16))
            Tac_ = Ex(nc.sbuf_tensor("Tac", [128, 2 * HB], BF16))
            Tac = lambda i: Tac_[:, i * HB:(i + 1) * HB]
            TacT_ = Ex(nc.sbuf_tensor("TacT", [128, 2 * HB], BF16))
            TacT = lambda i: TacT_[:, i * HB:(i + 1) * HB]
            Vb = Ex(nc.sbuf_tensor("Vb", [128, HB], BF16))
            VbT = Ex(nc.sbuf_tensor("VbT", [128, HB], BF16))
            rhs0 = Ex(nc.sbuf_tensor("rhs0", [128, HB], BF16))
            vnb = Ex(nc.sbuf_tensor("vnb", [128, HB], BF16))
            S_f = Ex(nc.sbuf_tensor("S_f", [128, HB], F32))
            S_b = Ex(nc.sbuf_tensor("S_b", [128, HB], BF16))

            gdma(out=tmasks_bf.ap().rearrange("p (k n) -> p k n", k=7), in_=tmasks_in.rearrange("k p n -> p k n"))
            gdma(out=irep_bf[:], in_=irep_in)
            scan_gl = gdma(out=tworep_bf[:], in_=tworep_in)
            tmask = lambda k: tmasks_bf[:, k * HB:(k + 1) * HB]
            hs = lambda buf, h: buf[:, h * 128:(h + 1) * 128]
            PSlo = PS[:, 0:HB]
            PShi = PS[:, 2048:2048 + HB]

            w_i1 = sv.inc(ve.memset(S_f[:], 0.0))
            se.wait_ge(sv.h, w_i1)
            w_i2 = ss.inc(se.copy(S_b[:], S_f[:]))
            te.wait_ge(ss.h, w_i2)
            te.wait_ge(gl.h, scan_gl)

            chunk_te_done = [0, 0]
            for kt in range(NT):
                tile_barrier()
                pb = kt % 2
                ld.inc(sy.dma_start(
                    out=kdm_c(pb).rearrange("p (t n) -> p t n", t=H), in_=kdm_sp[kt].rearrange("t p n -> p t n")), 16)
                ld.inc(sy.dma_start(
                    out=ktdm_c(pb).rearrange("p (t n) -> p t n", t=H), in_=ktdm_sp[kt].rearrange("t p n -> p t n")), 16)
                ld.inc(sy.dma_start(out=kttm_c(pb), in_=kttm_sp[kt]), 16)
                wl = ld.inc(sy.dma_start(out=vtm_c(pb), in_=vtm_sp[kt]), 16)
                te.wait_ge(ld.h, wl)
                ve.wait_ge(ld.h, wl)
                lastA = None
                for h in range(H):
                    b0, off = divmod(h * 128, 512)
                    lastA = te.matmul(psb(b0)[:, off:off + 128], hs(kdm_c(pb), h),
                                      hs(ktdm_c(pb), h), start=True, stop=True)
                w_A = sp.inc(lastA)
                ve.wait_ge(sp.h, w_A)
                w_G = sv.inc(ve.tensor_copy(Gsb[:], PSlo))
                lastm = None
                for k in range(7):
                    lastm = ve.tensor_tensor(Gm[:, k * HB:(k + 1) * HB], Gsb[:],
                                             tmask(k), ALU.mult)
                w_Gm = sv.inc(lastm)
                te.wait_ge(sv.h, w_Gm)
                sv.inc(ve.scalar_tensor_tensor(Vb[:], Gm[:, 0:HB], -1.0,
                                               irep_bf[:], ALU.mult, ALU.add))
                last0 = None
                for h in range(H):
                    b0, off = divmod(h * 128, 512)
                    last0 = te.matmul(psb(b0)[:, off:off + 128],
                                      Gm[:, h * 128:(h + 1) * 128],
                                      ident_bf[:], start=True, stop=True)
                w_l0 = sp.inc(last0)
                ve.wait_ge(sp.h, w_l0)
                w_v0t = sv.inc(ve.scalar_tensor_tensor(VbT[:], PSlo, -1.0,
                                                       irep_bf[:], ALU.mult, ALU.add))
                ve.tensor_copy(Tac(0), Vb[:])
                w_tc = sv.inc(ve.tensor_copy(TacT(0), VbT[:]))
                te.wait_ge(sv.h, w_tc)
                cur = 0
                for lv in range(1, 7):
                    gmk = Gm[:, lv * HB:(lv + 1) * HB]
                    lastD = None
                    for h in range(H):
                        b0, off = divmod(h * 128, 512)
                        te.matmul(psb(b0)[:, off:off + 128], hs(TacT(cur), h),
                                  gmk[:, h * 128:(h + 1) * 128], start=True, stop=False)
                        te.matmul(psb(b0)[:, off:off + 128], hs(TacT(cur), h),
                                  ident_bf[:], start=False, stop=True)
                        te.matmul(psb(4 + b0)[:, off:off + 128],
                                  gmk[:, h * 128:(h + 1) * 128],
                                  hs(TacT(cur), h), start=True, stop=False)
                        lastD = te.matmul(psb(4 + b0)[:, off:off + 128], ident_bf[:],
                                          hs(TacT(cur), h), start=False, stop=True)
                    w_D = sp.inc(lastD)
                    ve.wait_ge(sp.h, w_D)
                    se.wait_ge(sp.h, w_D)
                    w_V = sv.inc(ve.scalar_tensor_tensor(Vb[:], PSlo, -1.0,
                                                         tworep_bf[:], ALU.mult, ALU.add))
                    w_nT = ss.inc(se.activation(VbT[:], PShi, AF.Copy, bias=0.0, scale=-1.0))
                    ve.wait_ge(ss.h, w_nT)
                    w_VT = sv.inc(ve.tensor_tensor(VbT[:], VbT[:], tworep_bf[:], ALU.add))
                    te.wait_ge(sv.h, w_VT)
                    nxt = 1 - cur
                    lastT = None
                    for h in range(H):
                        b0, off = divmod(h * 128, 512)
                        te.matmul(psb(b0)[:, off:off + 128], hs(VbT, h), hs(Tac(cur), h),
                                  start=True, stop=True)
                        lastT = te.matmul(psb(4 + b0)[:, off:off + 128], hs(Tac(cur), h),
                                          hs(VbT, h), start=True, stop=True)
                    w_T = sp.inc(lastT)
                    ve.wait_ge(sp.h, w_T)
                    se.wait_ge(sp.h, w_T)
                    w_e1 = sv.inc(ve.tensor_copy(Tac(nxt), PSlo))
                    w_e2 = ss.inc(se.copy(TacT(nxt), PShi))
                    te.wait_ge(sv.h, w_e1)
                    te.wait_ge(ss.h, w_e2)
                    cur = nxt
                lastR = None
                for h in range(H):
                    b0, off = divmod(h * 128, 512)
                    lastR = te.matmul(psb(b0)[:, off:off + 128], hs(kdm_c(pb), h),
                                      hs(S_b, h), start=True, stop=True)
                w_R = sp.inc(lastR)
                ve.wait_ge(sp.h, w_R)
                w_r0 = sv.inc(ve.scalar_tensor_tensor(rhs0[:], PSlo, -1.0,
                                                      vtm_c(pb)[:, 0:HB], ALU.mult, ALU.add))
                te.wait_ge(sv.h, w_r0)
                lastV = None
                for h in range(H):
                    b0, off = divmod(h * 128, 512)
                    lastV = te.matmul(psb(4 + b0)[:, off:off + 128], hs(TacT(cur), h),
                                      hs(rhs0, h), start=True, stop=True)
                w_Vn = sp.inc(lastV)
                ve.wait_ge(sp.h, w_Vn)
                sy.wait_ge(so.h, so.n)
                w_vnb = sv.inc(ve.tensor_copy(vnb[:], PShi))
                te.wait_ge(sv.h, w_vnb)
                sy.wait_ge(sv.h, w_vnb)
                so.inc(sy.dma_start(
                    out=vn_sp[kt].rearrange("t p n -> p t n"), in_=vnb.ap().rearrange("p (t n) -> p t n", t=H)), 16)
                lastS = None
                for h in range(H):
                    b0, off = divmod(h * 128, 512)
                    lastS = te.matmul(psb(b0)[:, off:off + 128], hs(kttm_c(pb), h),
                                      hs(vnb, h), start=True, stop=True)
                w_S = sp.inc(lastS)
                ve.wait_ge(sp.h, w_S)
                w_Sa = sv.inc(ve.tensor_tensor(S_f[:], S_f[:], PSlo, ALU.add))
                se.wait_ge(sv.h, w_Sa)
                w_Sb = ss.inc(se.copy(S_b[:], S_f[:]))
                te.wait_ge(ss.h, w_Sb)
                chunk_te_done[pb] = sp.n
            sy.wait_ge(so.h, so.n)

        if stop_phase == "scan":
            raise _StopBuild
        with ExitStack() as po:
            Eo = po.enter_context
            outT = Eo(nc.sbuf_tensor("outT", [128, H * NQ], BF16))
            with ExitStack() as px:
                Ex = px.enter_context
                kth_ = Ex(nc.sbuf_tensor("kth", [128, 2 * N], BF16))
                kth = lambda s: kth_[:, s * N:(s + 1) * N]
                vnh_ = Ex(nc.sbuf_tensor("vnh", [128, 2 * N], BF16))
                vnh = lambda s: vnh_[:, s * N:(s + 1) * N]
                qh_ = Ex(nc.sbuf_tensor("qh", [128, 2 * NQ], BF16))
                qh = lambda s: qh_[:, s * NQ:(s + 1) * NQ]
                expb_ = Ex(nc.sbuf_tensor("expb", [128, 1024], BF16))
                expb = lambda i: expb_[:, i * 256:(i + 1) * 256]
                expm_ = Ex(nc.sbuf_tensor("expm", [128, 1024], BF16))
                expm = lambda i: expm_[:, i * 256:(i + 1) * 256]
                denf = Ex(nc.sbuf_tensor("denf", [1, 512], F32))
                denb = Ex(nc.sbuf_tensor("denb", [1, 512], BF16))
                dbc = Ex(nc.sbuf_tensor("dbc", [128, 512], BF16))
                pal_bf = Ex(nc.sbuf_tensor("pal_bf", [128, 4 * 256], BF16))
                pal_gl = gdma(out=pal_bf.ap().rearrange("p (k n) -> p k n", k=4),
                              in_=pal_in.rearrange("k p n -> p k n"))
                nc.vector.wait_ge(gl.h, pal_gl)

                att_done = [0, 0]
                av_free = [0, 0]; den_free = [0, 0]; dbc_free = [0, 0]
                bc_free = [0, 0]; db_free = [0, 0]

                def head_loads(h):
                    s = h % 2
                    sy.wait_ge(sp.h, att_done[s])
                    ld.inc(sy.dma_start(
                        out=kth(s).rearrange("p (t n) -> p t n", t=NT),
                        in_=ktdm_sp[:, h].rearrange("t p n -> p t n")), 16)
                    ld.inc(sy.dma_start(
                        out=vnh(s).rearrange("p (t n) -> p t n", t=NT),
                        in_=vn_sp[:, h].rearrange("t p n -> p t n")), 16)
                    return ld.inc(sy.dma_start(
                        out=qh(s).rearrange("p (t n) -> p t n", t=NQT),
                        in_=q_sp[:, h].rearrange("t p n -> p t n")), 16)

                def den_reg(mp):
                    return PS[0:1, 4 * 512 + mp * 256: 4 * 512 + mp * 256 + 256]

                def bc_reg(mp):
                    return PS[:, 5 * 512 + mp * 256: 5 * 512 + mp * 256 + 256]

                def sc_reg(sl):
                    return psb(sl, 256) if sl < 2 else psb(sl + 4, 256)

                def den_head(mm, w_pv):
                    mp = mm % 2
                    se.wait_ge(sp.h, w_pv)
                    se.wait_ge(sv.h, db_free[mp])
                    w_dn = ss.inc(se.copy(denf[:, mp * 256:mp * 256 + 256], den_reg(mp)))
                    den_free[mp] = w_dn
                    ve.wait_ge(ss.h, w_dn)
                    ve.reciprocal(denf[:, mp * 256:mp * 256 + 256],
                                  denf[:, mp * 256:mp * 256 + 256])
                    w_db = sv.inc(ve.tensor_copy(denb[:, mp * 256:mp * 256 + 256],
                                                 denf[:, mp * 256:mp * 256 + 256]))
                    db_free[mp] = w_db
                    return w_db

                def emit_chain(hh, mm, w_db):
                    mp = mm % 2
                    te.wait_ge(sv.h, w_db)
                    te.wait_ge(ss.h, bc_free[mp])
                    w_bc2 = sp.inc(te.matmul(bc_reg(mp), ones_row[:],
                                             denb[:, mp * 256:mp * 256 + 256],
                                             start=True, stop=True))
                    se.wait_ge(sp.h, w_bc2)
                    se.wait_ge(sv.h, dbc_free[mp])
                    w_dbc = ss.inc(se.copy(dbc[:, mp * 256:mp * 256 + 256], bc_reg(mp)))
                    bc_free[mp] = w_dbc
                    ve.wait_ge(ss.h, w_dbc)
                    w_o = sv.inc(ve.tensor_tensor(
                        outT[:, hh * NQ + mm * 256: hh * NQ + (mm + 1) * 256],
                        psb(2 + mp, 256), dbc[:, mp * 256:mp * 256 + 256], ALU.mult))
                    av_free[mp] = w_o
                    dbc_free[mp] = w_o

                sy.wait_ge(sv.h, sv.n)
                sy.wait_ge(ss.h, ss.n)
                sy.wait_ge(sp.h, sp.n)
                wlh = [0, 0]
                wlh[0] = head_loads(0)
                pend = None
                for h in range(H):
                    s = h % 2
                    te.wait_ge(ld.h, wlh[s])
                    se.wait_ge(ld.h, wlh[s])
                    if h + 1 < H:
                        wlh[1 - s] = head_loads(h + 1)
                    for m in range(QC):
                        mp = m % 2
                        nkt = 4 * m + 4
                        qrhs = qh(s)[:, m * 256:(m + 1) * 256]
                        w_scores = {}
                        for kt0 in range(min(4, nkt)):
                            w_scores[kt0] = sp.inc(te.matmul(
                                sc_reg(kt0 % 4),
                                kth(s)[:, kt0 * 128:(kt0 + 1) * 128],
                                qrhs, start=True, stop=True))
                        for kt in range(nkt):
                            sl = kt % 4
                            se.wait_ge(sp.h, w_scores[kt])
                            w_ex = ss.inc(se.activation(expb(sl), sc_reg(sl), AF.Exp,
                                                        bias=biasM[:, h:h + 1], scale=SCALE))
                            c = kt - 4 * m
                            if c >= 0:
                                ve.wait_ge(ss.h, w_ex)
                                w_mk = sv.inc(ve.tensor_tensor(
                                    expm(sl), expb(sl),
                                    pal_bf[:, c * 256:(c + 1) * 256], ALU.mult))
                                te.wait_ge(sv.h, w_mk)
                                src = expm(sl)
                            else:
                                te.wait_ge(ss.h, w_ex)
                                src = expb(sl)
                            if kt == 0:
                                te.wait_ge(sv.h, av_free[mp])
                                te.wait_ge(ss.h, den_free[mp])
                            te.matmul(psb(2 + mp, 256), vnh(s)[:, kt * 128:(kt + 1) * 128],
                                      src[:], start=(kt == 0), stop=(kt == nkt - 1))
                            lastP = te.matmul(den_reg(mp), ones_col[:],
                                              src[:], start=(kt == 0), stop=(kt == nkt - 1))
                            if kt + 4 < nkt:
                                w_scores[kt + 4] = sp.inc(te.matmul(
                                    sc_reg(sl),
                                    kth(s)[:, (kt + 4) * 128:(kt + 5) * 128],
                                    qrhs, start=True, stop=True))
                        w_pv = sp.inc(lastP)
                        w_db_m = den_head(m, w_pv)
                        if pend is not None:
                            emit_chain(*pend)
                        pend = (h, m, w_db_m)
                    att_done[s] = w_pv
                emit_chain(*pend)
                pend = None

            if stop_phase == "att":
                raise _StopBuild
            with ExitStack() as px:
                Ex = px.enter_context
                wobuf = Ex(nc.sbuf_tensor("wobuf", [128, DT * D], BF16))
                aodm = Ex(nc.sbuf_tensor("aodm", [128, DT * 1024], BF16))
                aotm = Ex(nc.sbuf_tensor("aotm", [128, D], BF16))
                xq2 = Ex(nc.sbuf_tensor("xq2", [128, D], F32))
                wo_done = gdma(
                    out=wobuf.ap().rearrange("p (k n) -> p k n", k=DT),
                    in_=wo_in.rearrange("(k p) n -> p k n", p=128))
                te.wait_ge(gl.h, wo_done)
                for th in range(NQ // 1024):
                    tile_barrier()
                    for dt in range(DT):
                        for c in range(2):
                            pbk = c % 2
                            lastW = None
                            for k in range(DT):
                                lastW = te.matmul(
                                    psb(pbk, 512),
                                    wobuf[:, k * D + dt * 128: k * D + dt * 128 + 128],
                                    outT[:, k * NQ + th * 1024 + c * 512:
                                         k * NQ + th * 1024 + c * 512 + 512],
                                    start=(k == 0), stop=(k == DT - 1))
                            w_w = sp.inc(lastW)
                            ve.wait_ge(sp.h, w_w)
                            w_ev = sv.inc(ve.tensor_copy(
                                aodm[:, dt * 1024 + c * 512: dt * 1024 + c * 512 + 512],
                                psb(pbk, 512)))
                            te.wait_ge(sv.h, w_ev)
                    for tt in range(8):
                        tile_barrier()
                        cols = [dt * 8 + tt for dt in range(DT)]
                        transposes(aodm, cols, aotm, 0, 2)
                        jt = th * 8 + tt
                        wxr = ld.inc(sy.dma_start(out=xq2[:],
                                                  in_=x_q[jt * 128:(jt + 1) * 128, :]), 16)
                        se.wait_ge(sv.h, sv.n)
                        ln_tile(aotm[:], sq_scr[:, 0:D])
                        ve.wait_ge(ld.h, wxr)
                        w_x1 = sv.inc(ve.tensor_tensor(xq2[:], xq2[:], sq_scr[:, 0:D],
                                                       ALU.add))
                        sy.wait_ge(sv.h, w_x1)
                        wst = so.inc(sy.dma_start(out=x1_sp[jt * 128:(jt + 1) * 128, :],
                                                  in_=xq2[:]), 16)
                        sy.wait_ge(so.h, wst)

        if stop_phase == "o":
            raise _StopBuild
        with ExitStack() as px:
            Ex = px.enter_context
            NHALF = NQ // 2
            HT = NHALF // 512
            zdm = Ex(nc.sbuf_tensor("zdm", [128, DT * NHALF], BF16))
            x1b = Ex(nc.sbuf_tensor("x1b", [128, D], F32))
            ztm = Ex(nc.sbuf_tensor("ztm", [128, D], BF16))
            hbuf = Ex(nc.sbuf_tensor("hbuf", [128, FT * NHALF], BF16))
            outacc = Ex(nc.sbuf_tensor("outacc", [128, DT * NHALF], F32))
            w1sl_ = Ex(nc.sbuf_tensor("w1sl", [128, 2 * DT * 128], BF16))
            w1sl = lambda i: w1sl_[:, i * DT * 128:(i + 1) * DT * 128]
            wgsl_ = Ex(nc.sbuf_tensor("wgsl", [128, 2 * DT * 128], BF16))
            wgsl = lambda i: wgsl_[:, i * DT * 128:(i + 1) * DT * 128]
            w2sl_ = Ex(nc.sbuf_tensor("w2sl", [128, 2 * FT * 128], BF16))
            w2sl = lambda i: w2sl_[:, i * FT * 128:(i + 1) * FT * 128]
            silub = Ex(nc.sbuf_tensor("silub", [128, 1024], BF16))
            tmulb = Ex(nc.sbuf_tensor("tmulb", [128, 1024], BF16))
            gbb = Ex(nc.sbuf_tensor("gbb", [128, NE], F32))
            bbb = Ex(nc.sbuf_tensor("bbb", [128, NE], F32))
            gsc = Ex(nc.sbuf_tensor("gsc", [128, 4 * NE + 4], F32))
            wtm = Ex(nc.sbuf_tensor("wtm", [128, NE], F32))
            wtmb = Ex(nc.sbuf_tensor("wtmb", [128, NE], BF16))
            wrow = Ex(nc.sbuf_tensor("wrow", [1, NE * NHALF], BF16))
            wbc = Ex(nc.sbuf_tensor("wbc", [128, NHALF], BF16))
            ffntm = Ex(nc.sbuf_tensor("ffntm", [128, D], BF16))

            if dbg:
                sy.wait_ge(gl.h, gl.n)
                wdg0 = so.inc(sy.dma_start(out=dbg_gw0[:, :], in_=gwbuf[:, :]), 16)
                sy.wait_ge(so.h, wdg0)
            ld.inc(sy.dma_start(out=gbb[:, 0:NE], in_=gateb_bc), 16)
            wbb_ld = ld.inc(sy.dma_start(out=bbb[:, 0:NE], in_=biases_bc), 16)

            def load_slice3(dst, dram3, e, colt, ktiles):
                src = dram3[e].rearrange("(k p) m -> p k m", p=128)[:, :, colt * 128:(colt + 1) * 128]
                return gdma(out=dst.rearrange("p (k n) -> p k n", k=ktiles), in_=src)

            def load_slice2(dst, dram2, colt, ktiles):
                src = dram2.rearrange("(k p) m -> p k m", p=128)[:, :, colt * 128:(colt + 1) * 128]
                return gdma(out=dst.rearrange("p (k n) -> p k n", k=ktiles), in_=src)

            slice_te_done = {}
            for hf in range(2):
                for i in range(NHALF // 128):
                    tile_barrier()
                    jt = hf * (NHALF // 128) + i
                    wx1 = ld.inc(sy.dma_start(out=x1b[:],
                                              in_=x1_sp[jt * 128:(jt + 1) * 128, :]), 16)
                    ve.wait_ge(ld.h, wx1)
                    se.wait_ge(ld.h, wx1)
                    ln_tile(x1b[:], ztm[:])
                    te.wait_ge(sv.h, sv.n)
                    done = 0
                    while done < DT:
                        grp = min(8, DT - done)
                        bank = (done // 8) % 2
                        lastT2 = None
                        for g2 in range(grp):
                            dtl = done + g2
                            lastT2 = te.transpose(
                                PSbf[:, bank * 1024 + g2 * 128: bank * 1024 + (g2 + 1) * 128],
                                ztm[:, dtl * 128:(dtl + 1) * 128], ident_bf[:])
                        w_t3 = sp.inc(lastT2)
                        ve.wait_ge(sp.h, w_t3)
                        lastV2 = None
                        for g2 in range(grp):
                            dtl = done + g2
                            lastV2 = ve.tensor_copy(
                                zdm[:, dtl * NHALF + i * 128: dtl * NHALF + (i + 1) * 128],
                                PSbf[:, bank * 1024 + g2 * 128: bank * 1024 + (g2 + 1) * 128])
                        w_v3 = sv.inc(lastV2)
                        te.wait_ge(sv.h, w_v3)
                        done += grp
                    te.wait_ge(gl.h, gl.n)
                    lastG = None
                    for k in range(DT):
                        lastG = te.matmul(psb(2, NE),
                                          zdm[:, k * NHALF + i * 128: k * NHALF + (i + 1) * 128],
                                          gwbuf[:, k * NE:(k + 1) * NE],
                                          start=(k == 0), stop=(k == DT - 1))
                    w_g = sp.inc(lastG)
                    ve.wait_ge(sp.h, w_g)
                    ve.wait_ge(ld.h, ld.n)
                    glog = gsc[:, 0:NE]
                    negmx = gsc[:, NE:NE + 1]
                    gexp = gsc[:, NE + 1:2 * NE + 1]
                    gsum = gsc[:, 2 * NE + 1:2 * NE + 2]
                    probas = gsc[:, 2 * NE + 2:3 * NE + 2]
                    biased = gsc[:, 3 * NE + 2:4 * NE + 2]
                    cnt = gsc[:, 4 * NE + 2:4 * NE + 3]
                    wsum = gsc[:, 4 * NE + 3:4 * NE + 4]
                    ve.tensor_tensor(glog, psb(2, NE), gbb[:, 0:NE], ALU.add)
                    w_gv = sv.inc(ve.tensor_reduce(negmx, glog, axis=AX.X, op=ALU.max,
                                                   negate=True))
                    te.wait_ge(sv.h, w_gv)
                    se.wait_ge(sv.h, w_gv)
                    w_ge = ss.inc(se.activation(gexp, glog, AF.Exp, bias=negmx, scale=1.0))
                    ve.wait_ge(ss.h, w_ge)
                    ve.tensor_reduce(gsum, gexp, axis=AX.X, op=ALU.add)
                    ve.reciprocal(gsum, gsum)
                    ve.tensor_scalar_mul(probas, gexp, gsum)
                    ve.tensor_tensor(biased, probas, bbb[:, 0:NE], ALU.add)
                    for e_ in range(NE):
                        ve.tensor_scalar(sq_scr[:, 0:NE], biased, biased[:, e_:e_ + 1],
                                         0.0, ALU.is_gt, ALU.add)
                        ve.tensor_reduce(cnt, sq_scr[:, 0:NE], axis=AX.X, op=ALU.add)
                        ve.tensor_scalar(wtm[:, e_:e_ + 1], cnt, float(TOPK) - 0.5, 1.0,
                                         ALU.is_lt, ALU.mult)
                    ve.tensor_tensor(wtm[:, 0:NE], wtm[:, 0:NE], probas, ALU.mult)
                    ve.tensor_reduce(wsum, wtm[:, 0:NE], axis=AX.X, op=ALU.add)
                    ve.reciprocal(wsum, wsum)
                    for e_ in range(NE):
                        ve.tensor_scalar_mul(wtm[:, e_:e_ + 1], wtm[:, e_:e_ + 1], wsum)
                    w_wt = sv.inc(ve.tensor_copy(wtmb[:, 0:NE], wtm[:, 0:NE]))
                    te.wait_ge(sv.h, w_wt)
                    if dbg and hf == 0:
                        sy.wait_ge(sv.h, w_wt)
                        so.inc(sy.dma_start(out=dbg_wtm[i], in_=wtm[:, 0:NE]), 16)
                        wdx = so.inc(sy.dma_start(out=dbg_gsc[i], in_=gsc[:, :]), 16)
                        sy.wait_ge(so.h, wdx)
                    w_tr = None
                    for e_ in range(NE):
                        w_tr = sp.inc(te.transpose(
                            PSbf[0:1, 2 * 1024 + e_ * 128: 2 * 1024 + (e_ + 1) * 128],
                            wtmb[:, e_:e_ + 1], ident_bf[:]))
                    ve.wait_ge(sp.h, w_tr)
                    w_wr = None
                    for e_ in range(NE):
                        w_wr = ve.tensor_copy(
                            wrow[0:1, e_ * NHALF + i * 128: e_ * NHALF + (i + 1) * 128],
                            PSbf[0:1, 2 * 1024 + e_ * 128: 2 * 1024 + (e_ + 1) * 128])
                    w_wr = sv.inc(w_wr)
                    te.wait_ge(sv.h, w_wr)
                w_zero = sv.inc(ve.memset(outacc[:], 0.0))
                te.wait_ge(sv.h, w_zero)

                def slice_src(dram, colt):
                    return dram.rearrange("(k p) m -> p k m", p=128)[:, :, colt * 128:(colt + 1) * 128]

                def issue_h(e2, ft):
                    pp = ft % 2
                    s = gsl[pp]
                    gp.wait_ge(s.h, s.n)
                    gp.wait_ge(sp.h, hguard[pp])
                    src1 = slice_src(ws1_in if e2 < 0 else w1_in[e2], ft)
                    srcg = slice_src(wsg_in if e2 < 0 else wg_in[e2], ft)
                    s.n = s.inc(gp.dma_start(
                        out=w1sl(pp).rearrange("p (k n) -> p k n", k=DT), in_=src1), 16)
                    gp.wait_ge(s.h, s.n)
                    s.n = s.inc(gp.dma_start(
                        out=wgsl(pp).rearrange("p (k n) -> p k n", k=DT), in_=srcg), 16)
                    return s.n

                def issue_o(e2, dt):
                    pp = dt % 2
                    s = gsl[pp]
                    gp.wait_ge(s.h, s.n)
                    gp.wait_ge(sp.h, oguard[pp])
                    src2 = slice_src(ws2_in if e2 < 0 else w2_in[e2], dt)
                    s.n = s.inc(gp.dma_start(
                        out=w2sl(pp).rearrange("p (k n) -> p k n", k=FT), in_=src2), 16)
                    return s.n

                hguard = [0, 0]
                oguard = [0, 0]
                hunit = [0]; ounit = [0]
                hq_done = {}; oq_done = {}
                pend = {}
                pend[("h", -1, 0)] = issue_h(-1, 0)
                for e_ in range(-1, NE):
                    if e_ >= 0:
                        se.wait_ge(sv.h, sv.n)
                        te.wait_ge(sv.h, sv.n)
                        for c in range(HT):
                            bnk = 6 + (c % 2)
                            w_b1 = sp.inc(te.matmul(psb(bnk, 512), ones_row[:],
                                                    wrow[0:1, e_ * NHALF + c * 512: e_ * NHALF + (c + 1) * 512],
                                                    start=True, stop=True))
                            se.wait_ge(sp.h, w_b1)
                            w_b2 = ss.inc(se.copy(wbc[:, c * 512:(c + 1) * 512], psb(bnk, 512)))
                            te.wait_ge(ss.h, w_b2)
                    for ft in range(FT):
                        pp = ft % 2
                        s = gsl[pp]
                        te.wait_ge(s.h, pend[("h", e_, ft)])
                        if ft + 1 < FT:
                            pend[("h", e_, ft + 1)] = issue_h(e_, ft + 1)
                        elif True:
                            pend[("o", e_, 0)] = issue_o(e_, 0)
                        for c in range(HT):
                            u = hunit[0]; hunit[0] += 1
                            b0, b1 = (0, 1) if u % 2 == 0 else (2, 3)
                            sl = u % 2
                            te.wait_ge(sv.h, hq_done.get(u - 2, 0))
                            lastH = None
                            for k in range(DT):
                                te.matmul(psb(b0, 512), w1sl(pp)[:, k * 128:(k + 1) * 128],
                                          zdm[:, k * NHALF + c * 512: k * NHALF + (c + 1) * 512],
                                          start=(k == 0), stop=(k == DT - 1))
                                lastH = te.matmul(psb(b1, 512), wgsl(pp)[:, k * 128:(k + 1) * 128],
                                                  zdm[:, k * NHALF + c * 512: k * NHALF + (c + 1) * 512],
                                                  start=(k == 0), stop=(k == DT - 1))
                            w_h = sp.inc(lastH)
                            se.wait_ge(sp.h, w_h)
                            w_si = ss.inc(se.activation(silub[:, sl * 512:sl * 512 + 512],
                                                        psb(b0, 512), AF.Silu,
                                                        bias=0.0, scale=1.0))
                            ve.wait_ge(ss.h, w_si)
                            w_m1 = sv.inc(ve.tensor_tensor(tmulb[:, sl * 512:sl * 512 + 512],
                                                           silub[:, sl * 512:sl * 512 + 512],
                                                           psb(b1, 512), ALU.mult))
                            if e_ >= 0:
                                w_m2 = sv.inc(ve.tensor_tensor(
                                    hbuf[:, ft * NHALF + c * 512: ft * NHALF + (c + 1) * 512],
                                    tmulb[:, sl * 512:sl * 512 + 512],
                                    wbc[:, c * 512:(c + 1) * 512], ALU.mult))
                            else:
                                w_m2 = sv.inc(ve.tensor_copy(
                                    hbuf[:, ft * NHALF + c * 512: ft * NHALF + (c + 1) * 512],
                                    tmulb[:, sl * 512:sl * 512 + 512]))
                            hq_done[u] = w_m2
                        hguard[pp] = sp.n
                    for dt in range(DT):
                        pp = dt % 2
                        s = gsl[pp]
                        if dt == 0:
                            te.wait_ge(sv.h, sv.n)
                        te.wait_ge(s.h, pend[("o", e_, dt)])
                        if dt + 1 < DT:
                            pend[("o", e_, dt + 1)] = issue_o(e_, dt + 1)
                        elif e_ + 1 < NE:
                            pend[("h", e_ + 1, 0)] = issue_h(e_ + 1, 0)
                        elif hf == 0:
                            pass
                        for c in range(HT):
                            u = ounit[0]; ounit[0] += 1
                            bo = 4 + (u % 4)
                            te.wait_ge(sv.h, oq_done.get(u - 3, 0))
                            lastO = None
                            for k in range(FT):
                                lastO = te.matmul(psb(bo, 512),
                                                  w2sl(pp)[:, k * 128:(k + 1) * 128],
                                                  hbuf[:, k * NHALF + c * 512: k * NHALF + (c + 1) * 512],
                                                  start=(k == 0), stop=(k == FT - 1))
                            w_o2 = sp.inc(lastO)
                            ve.wait_ge(sp.h, w_o2)
                            w_a = sv.inc(ve.tensor_tensor(
                                outacc[:, dt * NHALF + c * 512: dt * NHALF + (c + 1) * 512],
                                outacc[:, dt * NHALF + c * 512: dt * NHALF + (c + 1) * 512],
                                psb(bo, 512), ALU.add))
                            oq_done[u] = w_a
                        oguard[pp] = sp.n

                if dbg and hf == 0:
                    tile_barrier()
                    sy.wait_ge(sv.h, sv.n)
                    sy.wait_ge(ss.h, ss.n)
                    so.inc(sy.dma_start(out=dbg_gw[:, :], in_=gwbuf[:, :]), 16)
                    so.inc(sy.dma_start(out=dbg_hbuf[:, :], in_=hbuf[:, :]), 16)
                    so.inc(sy.dma_start(out=dbg_outacc[:, :], in_=outacc[:, :]), 16)
                    so.inc(sy.dma_start(out=dbg_wrow[:, :], in_=wrow[:, :]), 16)
                    wdbg = so.inc(sy.dma_start(out=dbg_wbc[:, :], in_=wbc[:, :]), 16)
                    sy.wait_ge(so.h, wdbg)
                w_cast = sv.inc(ve.tensor_copy(hbuf[:, 0:DT * NHALF], outacc[:, 0:DT * NHALF]))
                te.wait_ge(sv.h, w_cast)
                for i in range(NHALF // 128):
                    tile_barrier()
                    jt = hf * (NHALF // 128) + i
                    te.wait_ge(sv.h, sv.n)
                    done = 0
                    while done < DT:
                        grp = min(8, DT - done)
                        bank = (done // 8) % 2
                        lastT3 = None
                        for g2 in range(grp):
                            dtl = done + g2
                            lastT3 = te.transpose(
                                PSbf[:, bank * 1024 + g2 * 128: bank * 1024 + (g2 + 1) * 128],
                                hbuf[:, dtl * NHALF + i * 128: dtl * NHALF + (i + 1) * 128],
                                ident_bf[:])
                        w_t4 = sp.inc(lastT3)
                        ve.wait_ge(sp.h, w_t4)
                        w_v4 = sv.inc(ve.tensor_copy(
                            ffntm[:, done * 128:(done + grp) * 128],
                            PSbf[:, bank * 1024: bank * 1024 + grp * 128]))
                        te.wait_ge(sv.h, w_v4)
                        done += grp
                    wx1b = ld.inc(sy.dma_start(out=x1b[:],
                                               in_=x1_sp[jt * 128:(jt + 1) * 128, :]), 16)
                    se.wait_ge(sv.h, sv.n)
                    ln_tile(ffntm[:], sq_scr[:, 0:D])
                    ve.wait_ge(ld.h, wx1b)
                    w_out = sv.inc(ve.tensor_tensor(x1b[:], x1b[:], sq_scr[:, 0:D],
                                                    ALU.add))
                    sy.wait_ge(sv.h, w_out)
                    wsto = so.inc(sy.dma_start(out=out_sh[jt * 128:(jt + 1) * 128, :],
                                               in_=x1b[:]), 16)
                    sy.wait_ge(so.h, wsto)

        sy.wait_ge(so.h, so.n)
        gp.wait_ge(gl.h, gl.n)
      except _StopBuild:
        sy.wait_ge(so.h, so.n)
        gp.wait_ge(gl.h, gl.n)
    return nc


def _rope_tables(N):
    inv = 1.0 / ROPE_BASE ** (np.arange(0, HD, 2, dtype=np.float32) / HD)
    th = np.arange(N, dtype=np.float32)[:, None] * inv[None, :]
    return np.cos(th).astype(np.float32), np.sin(th).astype(np.float32)


def _host_prep(inputs, cfg, core):
    N, D, H, NE = cfg["N"], cfg["D"], cfg["H"], cfg["NE"]
    b, sub = core // 2, core % 2
    NQT = (N // 2) // 128
    qtiles = [2 * j + sub for j in range(NQT)]
    qidx = np.concatenate([np.arange(t * 128, (t + 1) * 128) for t in qtiles])

    x = np.asarray(inputs["L0_attn_x"][b], dtype=np.float32)
    d = {}
    d["x_full"] = np.ascontiguousarray(x)
    d["x_q"] = np.ascontiguousarray(x[qidx])
    for nm, key in [("wq_in", "L0_attn_Wq"), ("wk_in", "L0_attn_Wk"),
                    ("wv_in", "L0_attn_Wv"), ("wo_in", "L0_attn_Wo"),
                    ("wb_in", "L0_attn_Wb"), ("w1_in", "L0_ffn_W1"),
                    ("wg_in", "L0_ffn_Wg"), ("w2_in", "L0_ffn_W2"),
                    ("ws1_in", "L0_ffn_Ws1"), ("wsg_in", "L0_ffn_Wsg"),
                    ("ws2_in", "L0_ffn_Ws2"), ("gatew_in", "L0_ffn_gate_w")]:
        d[nm] = np.ascontiguousarray(np.asarray(inputs[key], dtype=np.float32))
    d["gateb_bc"] = np.ascontiguousarray(np.broadcast_to(
        np.asarray(inputs["L0_ffn_gate_b"], np.float32)[None, :], (128, NE)))
    d["biases_bc"] = np.ascontiguousarray(np.broadcast_to(
        np.asarray(inputs["L0_ffn_biases"], np.float32)[None, :], (128, NE)))
    coskv, sinkv = _rope_tables(N)
    coskv = np.ascontiguousarray(np.tile(coskv, (1, H)))
    sinkv = np.ascontiguousarray(np.tile(sinkv, (1, H)))
    d["cos_kv"] = coskv
    d["sin_kv"] = sinkv
    d["cos_q"] = np.ascontiguousarray(coskv[qidx])
    d["sin_q"] = np.ascontiguousarray(sinkv[qidx])
    d["ident_in"] = np.eye(128, dtype=np.float32)
    tril = np.tril(np.ones((128, 128), np.float32), -1)
    tm = np.zeros((7, 128, H * 128), np.float32)
    for k in range(7):
        bs = 2 ** (k + 1)
        blk = np.zeros((128, 128), np.float32)
        for bb in range(128 // bs):
            blk[bb * bs:(bb + 1) * bs, bb * bs:(bb + 1) * bs] = 1
        tm[k] = np.tile(tril * blk, (1, H))
    d["tmasks_in"] = tm
    d["irep_in"] = np.tile(np.eye(128, dtype=np.float32), (1, H))
    d["tworep_in"] = 2.0 * d["irep_in"]
    pal = np.zeros((4, 128, 256), np.float32)
    tri_kq = np.tril(np.ones((128, 128), np.float32)).T
    for c in range(4):
        for blk_ in range(2):
            rel = 2 * blk_ + sub - c
            if rel > 0:
                pal[c, :, blk_ * 128:(blk_ + 1) * 128] = 1.0
            elif rel == 0:
                pal[c, :, blk_ * 128:(blk_ + 1) * 128] = tri_kq
    d["pal_in"] = pal
    d["_qidx"] = qidx
    return d


def kernel(**inputs):
    cfg = FULL
    nc = build_program(cfg)
    in_maps, qidxs = [], []
    for core in range(8):
        d = _host_prep(inputs, cfg, core)
        qidxs.append(d.pop("_qidx"))
        in_maps.append(d)
    res = run_bass_kernel_spmd(nc, in_maps, core_ids=list(range(8)))
    x = np.asarray(inputs["L0_attn_x"], dtype=np.float32)
    out = np.zeros_like(x)
    for core in range(8):
        out[core // 2][qidxs[core]] = res.results[core]["out_sh"]
    return out
